# version 01
# baseline (speedup 1.0000x reference)
import numpy as np
import ml_dtypes
import concourse.bass as bass
import concourse.mybir as mybir
from concourse.bass_utils import run_bass_kernel_spmd
from contextlib import ExitStack

F32 = mybir.dt.float32
BF16 = mybir.dt.bfloat16
AF = mybir.ActivationFunctionType
ALU = mybir.AluOpType
AX = mybir.AxisListType

NCORES = 8
B_PER_CORE = 2
T = 2048
D = 2048
TOK = B_PER_CORE * T
EPS = 1e-5

_UID = [0]

def uname(base):
    _UID[0] += 1
    return f"{base}_{_UID[0]}"


class Ctr:
    def __init__(self, nc, name, stack):
        self.sem = stack.enter_context(nc.semaphore(uname(name)))
        self.val = 0

    def inc(self, inst, n=1):
        inst.then_inc(self.sem, n)
        self.val += n

    def wait(self, eng, val=None):
        eng.wait_ge(self.sem, self.val if val is None else val)


class Ctx:
    def __init__(self, nc, stack):
        self.nc = nc
        self.stack = stack
        self.pe = Ctr(nc, "c_pe", stack)
        self.act = Ctr(nc, "c_act", stack)
        self.dve = Ctr(nc, "c_dve", stack)
        self.pool = Ctr(nc, "c_pool", stack)
        self.store_ctrs = []

    def dma_ctr(self, name, store=False):
        c = Ctr(self.nc, "d_" + name, self.stack)
        if store:
            self.store_ctrs.append(c)
        return c

    def init_consts(self):
        self.eps_col = self.sb("eps_col", [128, 1], F32)
        self.dve.inc(self.nc.vector.memset(self.eps_col[:, 0:1], EPS), 1)

    def sb(self, name, shape, dtype):
        return self.stack.enter_context(self.nc.sbuf_tensor(uname(name), shape, dtype))

    def ps(self, name, shape, dtype=F32):
        return self.stack.enter_context(self.nc.psum_tensor(uname(name), shape, dtype))

    def final_barrier(self):
        for c in self.store_ctrs:
            c.wait(self.nc.gpsimd)


def V(cx, f, *a, **kw):
    cx.dve.wait(cx.nc.vector)
    cx.dve.inc(f(*a, **kw), 1)


def A(cx, f, *a, **kw):
    cx.act.wait(cx.nc.scalar)
    cx.act.inc(f(*a, **kw), 1)


def dma_in(cx, ctr, dst, src, eng=None):
    ctr.inc((eng or cx.nc.sync).dma_start(dst, src), 16)


def dma_out(cx, ctr, dst, src, eng=None):
    ctr.inc((eng or cx.nc.gpsimd).dma_start(dst, src), 16)


def load_col(cx, ctr, dst_col, src_1d, lo, n=128):
    dma_in(cx, ctr, dst_col, src_1d[lo:lo + n].unsqueeze(1))


def bcast_row(cx, ctr, dst, src_1d, width, parts=128):
    dma_in(cx, ctr, dst[0:parts, 0:width],
           src_1d[0:width].unsqueeze(0).partition_broadcast(parts))



def emit_prep_transpose(cx, w_src, wt_dst, R, C, ident_f32, sign=False,
                        colscale_sb=None):
    nc = cx.nc
    nR, nC = R // 128, C // 128
    with ExitStack() as st_:
        def lsb(name, shape, dtype):
            return st_.enter_context(nc.sbuf_tensor(uname(name), shape, dtype))
        wins = [lsb("prep_in", [128, C], F32) for _ in range(2)]
        wouts = [lsb("prep_out", [128, 128], BF16) for _ in range(2)]
        wmid = lsb("prep_mid", [128, 128], F32)
        pss = [st_.enter_context(nc.psum_tensor(uname("prep_ps"), [128, 128], F32))
               for _ in range(2)]
        in_ctr = [cx.dma_ctr("prep_in0"), cx.dma_ctr("prep_in1")]
        out_ctr = [cx.dma_ctr("prep_out0", store=True),
                   cx.dma_ctr("prep_out1", store=True)]
        for rc in range(nR):
            k = rc % 2
            if rc >= 2:
                cx.pe.wait(nc.sync)
            dma_in(cx, in_ctr[k], wins[k][:, :], w_src[rc * 128:(rc + 1) * 128, :])
            in_ctr[k].wait(nc.tensor)
            for cc in range(nC):
                p = pss[cc % 2]
                wo = wouts[cc % 2]
                oc = out_ctr[cc % 2]
                cx.dve.wait(nc.tensor)
                cx.act.wait(nc.tensor)
                cx.pe.inc(nc.tensor.transpose(p[:, :], wins[k][:, cc * 128:(cc + 1) * 128],
                                              ident_f32[:, :]), 1)
                if sign:
                    cx.pe.wait(nc.scalar)
                    cx.dve.wait(nc.scalar)
                    A(cx, nc.scalar.activation, wmid[:, :], p[:, :], AF.Sign,
                      bias=0.0, scale=1.0)
                    cx.act.wait(nc.vector)
                    oc.wait(nc.vector)
                    src = wmid
                else:
                    cx.pe.wait(nc.vector)
                    oc.wait(nc.vector)
                    src = p
                if colscale_sb is not None:
                    V(cx, nc.vector.tensor_tensor,
                      wo[:, :], src[:, :],
                      colscale_sb[:, rc * 128:(rc + 1) * 128], ALU.mult)
                else:
                    V(cx, nc.vector.tensor_copy, wo[:, :], src[:, :])
                cx.dve.wait(nc.gpsimd)
                dma_out(cx, oc, wt_dst[cc, rc], wo[:, :])
        for oc in out_ctr:
            oc.wait(nc.sync)
            oc.wait(nc.gpsimd)



def emit_ln_stats(cx, xt, n_free, stp, sq_scratch, affine_out_bf16=None,
                  out_f32=None, rms_only=False):
    nc = cx.nc
    s1, s2, mu, var, rstd = (stp[k] for k in ("s1", "s2", "mu", "var", "rstd"))
    inv = 1.0 / n_free
    cx.dve.wait(nc.scalar)
    A(cx, nc.scalar.activation, sq_scratch[:, :], xt[:, :], AF.Square,
      bias=0.0, scale=1.0)
    cx.act.wait(nc.vector)
    V(cx, nc.vector.reduce_sum, s2[:, 0:1], sq_scratch[:, :], axis=AX.X)
    if rms_only:
        V(cx, nc.vector.tensor_scalar, var[:, 0:1], s2[:, 0:1], inv, None,
          ALU.mult)
    else:
        V(cx, nc.vector.reduce_sum, s1[:, 0:1], xt[:, :], axis=AX.X)
        V(cx, nc.vector.tensor_scalar, mu[:, 0:1], s1[:, 0:1], inv, None,
          ALU.mult)
        cx.dve.wait(nc.scalar)
        A(cx, nc.scalar.activation, s1[:, 0:1], mu[:, 0:1], AF.Square,
          bias=0.0, scale=1.0)
        cx.act.wait(nc.vector)
        V(cx, nc.vector.tensor_scalar, var[:, 0:1], s2[:, 0:1], inv, None,
          ALU.mult)
        V(cx, nc.vector.tensor_tensor, var[:, 0:1], var[:, 0:1], s1[:, 0:1],
          ALU.subtract)
    cx.dve.wait(nc.scalar)
    A(cx, nc.scalar.activation, s2[:, 0:1], var[:, 0:1], AF.Sqrt,
      bias=stp["eps"][:, 0:1], scale=1.0)
    cx.act.wait(nc.vector)
    V(cx, nc.vector.reciprocal, rstd[:, 0:1], s2[:, 0:1])
    for out in (affine_out_bf16, out_f32):
        if out is None:
            continue
        if rms_only:
            V(cx, nc.vector.tensor_scalar, out[:, :], xt[:, :], rstd[:, 0:1],
              None, ALU.mult)
        else:
            V(cx, nc.vector.tensor_scalar, out[:, :], xt[:, :], mu[:, 0:1],
              rstd[:, 0:1], ALU.subtract, ALU.mult)


def mk_stat_pool(cx, prefix):
    d = {k: cx.sb(f"{prefix}_{k}", [128, 1], F32)
         for k in ("s1", "s2", "mu", "var", "rstd")}
    d["eps"] = cx.eps_col
    return d



def emit_ffn_segment(cx, x_in, x_out, wcfg, ident_f32, ident_bf16,
                     debug_stop=None):
    nc = cx.nc
    I = wcfg["I"]
    nI = I // 128
    swiglu = wcfg["kind"] == "swiglu"

    c_const = cx.dma_ctr("ffn_const")

    ds_b = cx.sb("ffn_ds_b", [128, D], F32)
    bcast_row(cx, c_const, ds_b, wcfg["down_s"], D)
    if wcfg.get("out_s") is not None:
        os_c = cx.sb("ffn_os_c", [128, 1], F32)
        dma_in(cx, c_const, os_c[:, 0:1],
               wcfg["out_s"][0:1].unsqueeze(0).partition_broadcast(128))
        c_const.wait(nc.vector)
        V(cx, nc.vector.tensor_scalar, ds_b[:, :], ds_b[:, :], os_c[:, 0:1],
          None, ALU.mult)
    c_const.wait(nc.vector)

    sgn = not swiglu
    if swiglu:
        emit_prep_transpose(cx, wcfg["gate_w"], wcfg["gwT"], I, D, ident_f32)
    emit_prep_transpose(cx, wcfg["up_w"], wcfg["uwT"], I, D, ident_f32, sign=sgn)
    emit_prep_transpose(cx, wcfg["down_w"], wcfg["dwT"], D, I, ident_f32,
                        sign=sgn, colscale_sb=ds_b)

    gs_cols = cx.sb("ffn_gs", [128, nI], F32) if swiglu else None
    us_cols = cx.sb("ffn_us", [128, nI], F32)
    for ic in range(nI):
        if swiglu:
            load_col(cx, c_const, gs_cols[:, ic:ic + 1], wcfg["gate_s"], ic * 128)
        load_col(cx, c_const, us_cols[:, ic:ic + 1], wcfg["up_s"], ic * 128)
    c_const.wait(nc.scalar)
    c_const.wait(nc.vector)

    xt = cx.sb("ffn_xt", [128, D], F32)
    c_xt = cx.dma_ctr("ffn_xt")
    sq = cx.sb("ffn_sq", [128, D], F32)
    xn_tm = cx.sb("ffn_xntm", [128, D], BF16)
    xn_fm = cx.sb("ffn_xnfm", [128, 16 * 512], BF16)
    h_fm = cx.sb("ffn_hfm", [128, nI * 512], BF16)
    gu_w = [[cx.sb(f"ffn_gw", [128, 16 * 128], BF16),
             cx.sb(f"ffn_uw", [128, 16 * 128], BF16)] for _ in range(2)]
    c_guw = [cx.dma_ctr("ffn_guw0"), cx.dma_ctr("ffn_guw1")]
    dw_sb = cx.sb("ffn_dw", [128, nI * 512], BF16)
    c_dw = cx.dma_ctr("ffn_dw")
    sg = cx.sb("ffn_sg", [128, 512], BF16)
    uu = cx.sb("ffn_uu", [128, 512], BF16)
    res = [cx.sb(f"ffn_res{i}", [128, 512], F32) for i in range(2)]
    c_res = [cx.dma_ctr("ffn_res0"), cx.dma_ctr("ffn_res1")]
    out_t = [cx.sb(f"ffn_out{i}", [128, 512], F32) for i in range(2)]
    c_out = [cx.dma_ctr("ffn_out0", store=True), cx.dma_ctr("ffn_out1", store=True)]
    c_dbg = cx.dma_ctr("ffn_dbg", store=True) if debug_stop else None
    stp = mk_stat_pool(cx, "ffn")

    ps_t = [cx.ps(f"ffn_pst{i}", [128, 128], BF16) for i in range(2)]
    ps_g = cx.ps("ffn_psg", [128, 512])
    ps_u = cx.ps("ffn_psu", [128, 512])
    ps_d = [cx.ps(f"ffn_psd{i}", [128, 512]) for i in range(2)]

    n_tc = TOK // 512
    for tc in range(n_tc):
        for ts in range(4):
            row0 = tc * 512 + ts * 128
            cx.dve.wait(nc.sync)
            cx.act.wait(nc.sync)
            dma_in(cx, c_xt, xt[:, :], x_in[row0:row0 + 128, :])
            c_xt.wait(nc.vector)
            c_xt.wait(nc.scalar)
            emit_ln_stats(cx, xt, D, stp, sq, affine_out_bf16=xn_tm)
            for dc in range(16):
                p = ps_t[dc % 2]
                cx.dve.wait(nc.tensor)
                cx.pe.inc(nc.tensor.transpose(
                    p[:, :], xn_tm[:, dc * 128:(dc + 1) * 128], ident_bf16[:, :]), 1)
                cx.pe.wait(nc.vector)
                V(cx, nc.vector.tensor_copy,
                  xn_fm[:, dc * 512 + ts * 128: dc * 512 + (ts + 1) * 128],
                  p[:, :])

        if debug_stop == "ln":
            cx.dve.wait(nc.gpsimd)
            dma_out(cx, c_dbg, wcfg["dbg"][tc * 128:(tc + 1) * 128, :], xn_fm[:, :])
            c_dbg.wait(nc.vector)
            continue

        for ic in range(nI):
            k = ic % 2
            gw, uw = gu_w[k]
            if ic >= 2:
                cx.pe.wait(nc.sync)
            if swiglu:
                dma_in(cx, c_guw[k], gw[:, :].rearrange("p (c i) -> p c i", c=16),
                       wcfg["gwT"][:, ic].transpose([1, 0, 2]))
            dma_in(cx, c_guw[k], uw[:, :].rearrange("p (c i) -> p c i", c=16),
                   wcfg["uwT"][:, ic].transpose([1, 0, 2]))
            c_guw[k].wait(nc.tensor)
            cx.act.wait(nc.tensor)
            cx.dve.wait(nc.tensor)
            for dc in range(16):
                last = (dc == 15)
                if swiglu:
                    m = nc.tensor.matmul(ps_g[:, :], gw[:, dc * 128:(dc + 1) * 128],
                                         xn_fm[:, dc * 512:(dc + 1) * 512],
                                         start=(dc == 0), stop=last)
                    if last:
                        cx.pe.inc(m, 1)
                m = nc.tensor.matmul(ps_u[:, :], uw[:, dc * 128:(dc + 1) * 128],
                                     xn_fm[:, dc * 512:(dc + 1) * 512],
                                     start=(dc == 0), stop=last)
                if last:
                    cx.pe.inc(m, 1)
            cx.pe.wait(nc.scalar)
            cx.pe.wait(nc.vector)
            if swiglu:
                cx.dve.wait(nc.scalar)
                A(cx, nc.scalar.activation, sg[:, :], ps_g[:, :], AF.Silu,
                  bias=0.0, scale=gs_cols[:, ic:ic + 1])
                V(cx, nc.vector.tensor_scalar, uu[:, :], ps_u[:, :],
                  us_cols[:, ic:ic + 1], None, ALU.mult)
                cx.act.wait(nc.vector)
                V(cx, nc.vector.tensor_tensor,
                  h_fm[:, ic * 512:(ic + 1) * 512], sg[:, :], uu[:, :], ALU.mult)
            else:
                A(cx, nc.scalar.activation,
                  h_fm[:, ic * 512:(ic + 1) * 512], ps_u[:, :], AF.Relu,
                  bias=0.0, scale=us_cols[:, ic:ic + 1])

        if debug_stop == "gu":
            cx.dve.wait(nc.gpsimd)
            cx.act.wait(nc.gpsimd)
            dma_out(cx, c_dbg, wcfg["dbg"][tc * 128:(tc + 1) * 128, :],
                    h_fm[:, 0:8192])
            c_dbg.wait(nc.vector)
            c_dbg.wait(nc.scalar)
            continue

        for do in range(4):
            cx.pe.wait(nc.sync)
            for o in range(4):
                dma_in(cx, c_dw,
                       dw_sb[:, :].rearrange("p (c o f) -> p c o f", c=nI, o=4)[:, :, o, :],
                       wcfg["dwT"][:, do * 4 + o].transpose([1, 0, 2]))
            c_dw.wait(nc.tensor)
            for ts in range(4):
                row0 = tc * 512 + ts * 128
                k = ts % 2
                p = ps_d[k]
                cx.dve.wait(nc.sync)
                dma_in(cx, c_res[k], res[k][:, :],
                       x_in[row0:row0 + 128, do * 512:(do + 1) * 512])
                cx.dve.wait(nc.tensor)
                for ic in range(nI):
                    m = nc.tensor.matmul(p[:, :],
                                         h_fm[:, ic * 512 + ts * 128: ic * 512 + (ts + 1) * 128],
                                         dw_sb[:, ic * 512:(ic + 1) * 512],
                                         start=(ic == 0), stop=(ic == nI - 1))
                    if ic == nI - 1:
                        cx.pe.inc(m, 1)
                cx.pe.wait(nc.vector)
                c_res[k].wait(nc.vector)
                c_out[k].wait(nc.vector)
                V(cx, nc.vector.tensor_tensor,
                  out_t[k][:, :], p[:, :], res[k][:, :], ALU.add)
                cx.dve.wait(nc.gpsimd)
                dma_out(cx, c_out[k],
                        x_out[row0:row0 + 128, do * 512:(do + 1) * 512],
                        out_t[k][:, :])



def host_constants():
    return {
        "ident": np.eye(128, dtype=np.float32),
        "identb": np.eye(128, dtype=np.float32).astype(ml_dtypes.bfloat16),
    }


def declare_ffn_io(nc, layer, prefix=""):
    q = prefix
    if layer == 0:
        I = 5632
        d = {
            "I": I, "kind": "swiglu",
            "gate_w": nc.dram_tensor(q + "gate_w", [I, D], F32, kind="ExternalInput").ap(),
            "up_w": nc.dram_tensor(q + "up_w", [I, D], F32, kind="ExternalInput").ap(),
            "down_w": nc.dram_tensor(q + "down_w", [D, I], F32, kind="ExternalInput").ap(),
            "gate_s": nc.dram_tensor(q + "gate_s", [I], F32, kind="ExternalInput").ap(),
            "up_s": nc.dram_tensor(q + "up_s", [I], F32, kind="ExternalInput").ap(),
            "down_s": nc.dram_tensor(q + "down_s", [D], F32, kind="ExternalInput").ap(),
            "out_s": None,
            "gwT": nc.dram_tensor(q + "gwT", [D // 128, I // 128, 128, 128], BF16).ap(),
        }
    else:
        I = 8192
        d = {
            "I": I, "kind": "signrelu",
            "up_w": nc.dram_tensor(q + "up_w", [I, D], F32, kind="ExternalInput").ap(),
            "down_w": nc.dram_tensor(q + "down_w", [D, I], F32, kind="ExternalInput").ap(),
            "up_s": nc.dram_tensor(q + "up_s", [I], F32, kind="ExternalInput").ap(),
            "down_s": nc.dram_tensor(q + "down_s", [D], F32, kind="ExternalInput").ap(),
            "out_s": nc.dram_tensor(q + "out_s", [1], F32, kind="ExternalInput").ap(),
        }
    d["uwT"] = nc.dram_tensor(q + "uwT", [D // 128, I // 128, 128, 128], BF16).ap()
    d["dwT"] = nc.dram_tensor(q + "dwT", [I // 128, D // 128, 128, 128], BF16).ap()
    return d


def build_ffn_program(layer, debug_stop=None):
    nc = bass.Bass("TRN2", target_bir_lowering=False)
    x_in = nc.dram_tensor("x_in", [TOK, D], F32, kind="ExternalInput").ap()
    x_out = nc.dram_tensor("x_out", [TOK, D], F32, kind="ExternalOutput").ap()
    ident = nc.dram_tensor("ident", [128, 128], F32, kind="ExternalInput").ap()
    identb = nc.dram_tensor("identb", [128, 128], BF16, kind="ExternalInput").ap()
    wcfg = declare_ffn_io(nc, layer)
    if debug_stop:
        wcfg["dbg"] = nc.dram_tensor("dbg", [TOK // 4, 8192], BF16,
                                     kind="ExternalOutput").ap()
    with ExitStack() as stack:
        cx = Ctx(nc, stack)
        cx.init_consts()
        ident_sb = cx.sb("ident_sb", [128, 128], F32)
        identb_sb = cx.sb("identb_sb", [128, 128], BF16)
        c0 = cx.dma_ctr("ident")
        dma_in(cx, c0, ident_sb[:, :], ident[:, :])
        dma_in(cx, c0, identb_sb[:, :], identb[:, :])
        c0.wait(nc.tensor)
        emit_ffn_segment(cx, x_in, x_out, wcfg, ident_sb, identb_sb,
                         debug_stop=debug_stop)
        cx.final_barrier()
    return nc





def emit_prep_cast(cx, w_src, wb_dst, R, C):
    nc = cx.nc
    nR = R // 128
    with ExitStack() as st_:
        wins = [st_.enter_context(nc.sbuf_tensor(uname("prepc_in"), [128, C], F32))
                for _ in range(2)]
        wos = [st_.enter_context(nc.sbuf_tensor(uname("prepc_out"), [128, C], BF16))
               for _ in range(2)]
        in_ctr = [cx.dma_ctr("prepc_in0"), cx.dma_ctr("prepc_in1")]
        out_ctr = [cx.dma_ctr("prepc_out0", store=True),
                   cx.dma_ctr("prepc_out1", store=True)]
        for rc in range(nR):
            k = rc % 2
            if rc >= 2:
                cx.act.wait(nc.sync)
            dma_in(cx, in_ctr[k], wins[k][:, :], w_src[rc * 128:(rc + 1) * 128, :])
            in_ctr[k].wait(nc.scalar)
            out_ctr[k].wait(nc.scalar)
            A(cx, nc.scalar.activation, wos[k][:, :], wins[k][:, :], AF.Copy,
              bias=0.0, scale=1.0)
            cx.act.wait(nc.gpsimd)
            dma_out(cx, out_ctr[k], wb_dst[rc * 128:(rc + 1) * 128, :], wos[k][:, :])
        for oc in out_ctr:
            oc.wait(nc.sync)
            oc.wait(nc.gpsimd)



NUM_HEADS = 16
HEAD_DIM = 128
CHUNK = 128
NCHUNK = T // CHUNK


def attn_host_constants():
    import math
    H = NUM_HEADS
    start = 2.0 ** (-2.0 ** (-(math.log2(H) - 3)))
    slopes = np.array([start * start ** i for i in range(H)], dtype=np.float32)
    r = np.exp(-slopes).astype(np.float32)

    i = np.arange(CHUNK, dtype=np.float32)
    qdec = r[:, None] ** (i[None, :] + 1.0)
    kdec = r[:, None] ** (CHUNK - 1.0 - i[None, :])
    diff = i[:, None] - i[None, :]
    mask = diff >= 0
    dmask = np.where(mask[None], r[:, None, None] ** np.where(mask, diff, 0.0)[None], 0.0)
    r_C = (r ** float(CHUNK)).astype(np.float64)

    bf = ml_dtypes.bfloat16
    c = {}
    c["qdecT"] = np.broadcast_to(qdec.reshape(1, H * CHUNK), (128, H * CHUNK)).astype(bf)
    c["kdecT"] = np.broadcast_to(kdec.reshape(1, H * CHUNK), (128, H * CHUNK)).astype(bf)
    c["dmaskT"] = dmask.transpose(2, 0, 1).reshape(CHUNK, H * CHUNK).astype(bf)
    c["r_C"] = [float(v) for v in r_C]
    DH = HEAD_DIM
    freq = (1.0 / 1024.0) ** np.linspace(0.0, 1.0, DH // 4, dtype=np.float32)
    freq = np.concatenate([freq, np.zeros(DH // 4, dtype=np.float32)])
    t = np.arange(T, dtype=np.float32)
    theta = t[None, :] * freq[:, None]
    cosT = np.cos(theta)
    sinT = np.sin(theta)
    c["cosT"] = np.concatenate([cosT, cosT], 0).astype(bf)
    c["sinS"] = np.concatenate([sinT, -sinT], 0).astype(bf)
    psw = np.zeros((128, 128), dtype=np.float32)
    for m_ in range(128):
        psw[(m_ + 64) % 128, m_] = 1.0
    c["Pswap"] = psw.astype(bf)
    return c


def declare_attn_io(nc, prefix=""):
    q = prefix
    d = {
        "Wq": nc.dram_tensor(q + "Wq", [D, D], F32, kind="ExternalInput").ap(),
        "Wk": nc.dram_tensor(q + "Wk", [D, D], F32, kind="ExternalInput").ap(),
        "Wv": nc.dram_tensor(q + "Wv", [D, D], F32, kind="ExternalInput").ap(),
        "Wo": nc.dram_tensor(q + "Wo", [D, D], F32, kind="ExternalInput").ap(),
        "rms_w": nc.dram_tensor(q + "rms_w", [D], F32, kind="ExternalInput").ap(),
        "qdecT": nc.dram_tensor(q + "qdecT", [128, NUM_HEADS * 128], BF16, kind="ExternalInput").ap(),
        "kdecT": nc.dram_tensor(q + "kdecT", [128, NUM_HEADS * 128], BF16, kind="ExternalInput").ap(),
        "dmaskT": nc.dram_tensor(q + "dmaskT", [128, NUM_HEADS * 128], BF16, kind="ExternalInput").ap(),
        "cosT": nc.dram_tensor(q + "cosT", [128, T], BF16, kind="ExternalInput").ap(),
        "sinS": nc.dram_tensor(q + "sinS", [128, T], BF16, kind="ExternalInput").ap(),
        "Pswap": nc.dram_tensor(q + "Pswap", [128, 128], BF16, kind="ExternalInput").ap(),
        "WqB": nc.dram_tensor(q + "WqB", [D, D], BF16).ap(),
        "WkB": nc.dram_tensor(q + "WkB", [D, D], BF16).ap(),
        "WvB": nc.dram_tensor(q + "WvB", [D, D], BF16).ap(),
        "WoB": nc.dram_tensor(q + "WoB", [D, D], BF16).ap(),
    }
    return d


def emit_attn_segment(cx, x_in, x_out, w, r_C, ident_bf16):
    nc = cx.nc
    H = NUM_HEADS

    emit_prep_cast(cx, w["Wq"], w["WqB"], D, D)
    emit_prep_cast(cx, w["Wk"], w["WkB"], D, D)
    emit_prep_cast(cx, w["Wv"], w["WvB"], D, D)
    emit_prep_cast(cx, w["Wo"], w["WoB"], D, D)

    c_const = cx.dma_ctr("attn_const")
    qdec_h = cx.sb("qdec_h", [128, 128], BF16)
    kdec_h = cx.sb("kdec_h", [128, 128], BF16)
    dmask_h = cx.sb("dmask_h", [128, 128], BF16)
    c_dec = cx.dma_ctr("attn_dec")
    cosT = cx.sb("cosT", [128, T], BF16)
    sinS = cx.sb("sinS", [128, T], BF16)
    pswap = cx.sb("pswap", [128, 128], BF16)
    rmsw_b = cx.sb("rmsw_b", [128, D], BF16)
    dma_in(cx, c_const, cosT[:, :], w["cosT"][:, :])
    dma_in(cx, c_const, sinS[:, :], w["sinS"][:, :])
    dma_in(cx, c_const, pswap[:, :], w["Pswap"][:, :])
    c_const.wait(nc.tensor)
    c_rmsw = cx.dma_ctr("attn_rmsw")
    dma_in(cx, c_rmsw, rmsw_b[0:128, 0:D],
           w["rms_w"][0:D].unsqueeze(0).partition_broadcast(128), eng=nc.gpsimd)
    c_rmsw.wait(nc.vector)
    c_const.wait(nc.vector)
    c_const.wait(nc.scalar)

    nTC = T // 128
    nSL = T // 512
    big_fm = cx.sb("a_bigfm", [128, 16 * T], BF16)
    attn_tm = cx.sb("a_attntm", [128, nTC * D], BF16)
    xt = cx.sb("a_xt", [128, D], F32)
    c_xt = cx.dma_ctr("a_xt")
    sq = cx.sb("a_sq", [128, D], F32)
    xn_tm = cx.sb("a_xntm", [128, D], BF16)
    stp = mk_stat_pool(cx, "a_ln")

    qf = cx.sb("a_qf", [128, T], BF16)
    kf = cx.sb("a_kf", [128, T], BF16)
    vt = cx.sb("a_vt", [128, T], BF16)
    rot_t = cx.sb("a_rott", [128, 512], BF16)
    qraw = cx.sb("a_qraw", [128, 512], BF16)
    wslab = cx.sb("a_wslab", [128, 16 * 512], BF16)
    whead = [wslab[:, i * 2048:(i + 1) * 2048] for i in range(3)]
    c_wh = cx.dma_ctr("a_wh")
    kv_f = [cx.sb(f"a_kvf{i}", [128, 128], F32) for i in range(2)]
    kv_b = cx.sb("a_kvb", [128, 128], BF16)
    qd_t = cx.sb("a_qd", [128, 128], BF16)
    kd_t = cx.sb("a_kd", [128, 128], BF16)
    kdtm = cx.sb("a_kdtm", [128, 128], BF16)
    sT_t = cx.sb("a_sT", [128, 128], BF16)
    wo_sl = wslab
    c_wo = cx.dma_ctr("a_wo")
    res = [cx.sb(f"a_res{i}", [128, 512], F32) for i in range(2)]
    c_res = [cx.dma_ctr("a_res0"), cx.dma_ctr("a_res1")]
    out_t = [cx.sb(f"a_out{i}", [128, 512], F32) for i in range(2)]
    c_out = [cx.dma_ctr("a_out0", store=True), cx.dma_ctr("a_out1", store=True)]

    ps_p = cx.ps("a_psp", [128, 512])
    ps_v = cx.ps("a_psv", [128, 128])
    ps_sc = cx.ps("a_pssc", [128, 128])
    ps_at = cx.ps("a_psat", [128, 128])
    ps_tr = cx.ps("a_pstr", [128, 128], BF16)
    ps_kv = cx.ps("a_pskv", [128, 128])

    for b in range(B_PER_CORE):
        base = b * T
        for ts in range(nTC):
            row0 = base + ts * 128
            cx.dve.wait(nc.sync)
            cx.act.wait(nc.sync)
            dma_in(cx, c_xt, xt[:, :], x_in[row0:row0 + 128, :])
            c_xt.wait(nc.vector)
            c_xt.wait(nc.scalar)
            emit_ln_stats(cx, xt, D, stp, sq, affine_out_bf16=xn_tm)
            for dc in range(16):
                cx.dve.wait(nc.tensor)
                cx.pe.inc(nc.tensor.transpose(
                    ps_tr[:, :], xn_tm[:, dc * 128:(dc + 1) * 128], ident_bf16[:, :]), 1)
                cx.pe.wait(nc.vector)
                V(cx, nc.vector.tensor_copy,
                  big_fm[:, dc * T + ts * 128: dc * T + (ts + 1) * 128], ps_tr[:, :])

        for h in range(H):
            cx.pe.wait(nc.sync)
            cx.dve.wait(nc.sync)
            dma_in(cx, c_dec, qdec_h[:, :], w["qdecT"][:, h * 128:(h + 1) * 128])
            dma_in(cx, c_dec, kdec_h[:, :], w["kdecT"][:, h * 128:(h + 1) * 128])
            dma_in(cx, c_dec, dmask_h[:, :], w["dmaskT"][:, h * 128:(h + 1) * 128])
            for wi, wname in enumerate(("WqB", "WkB", "WvB")):
                dma_in(cx, c_wh,
                       whead[wi][:, :].rearrange("p (c e) -> p c e", c=16),
                       w[wname][:, h * 128:(h + 1) * 128]
                       .rearrange("(c p) e -> c p e", p=128).transpose([1, 0, 2]))
            c_wh.wait(nc.tensor)

            for which, dst in ((0, qf), (1, kf)):
                for sl in range(nSL):
                    t0 = sl * 512
                    cx.dve.wait(nc.tensor)
                    for dc in range(16):
                        m = nc.tensor.matmul(
                            ps_p[:, :], whead[which][:, dc * 128:(dc + 1) * 128],
                            big_fm[:, dc * T + t0: dc * T + t0 + 512],
                            start=(dc == 0), stop=(dc == 15))
                        if dc == 15:
                            cx.pe.inc(m, 1)
                    cx.pe.wait(nc.vector)
                    V(cx, nc.vector.tensor_copy, qraw[:, :], ps_p[:, :])
                    cx.dve.wait(nc.tensor)
                    cx.pe.inc(nc.tensor.matmul(ps_p[:, :], pswap[:, :],
                                               qraw[:, :], start=True, stop=True), 1)
                    cx.pe.wait(nc.vector)
                    V(cx, nc.vector.tensor_tensor, rot_t[:, :],
                      ps_p[:, :], sinS[:, t0:t0 + 512], ALU.mult)
                    V(cx, nc.vector.tensor_tensor, dst[:, t0:t0 + 512],
                      qraw[:, :], cosT[:, t0:t0 + 512], ALU.mult)
                    V(cx, nc.vector.tensor_tensor, dst[:, t0:t0 + 512],
                      dst[:, t0:t0 + 512], rot_t[:, :], ALU.add)

            for tcc in range(nTC):
                cx.dve.wait(nc.tensor)
                for dc in range(16):
                    m = nc.tensor.matmul(
                        ps_v[:, :], big_fm[:, dc * T + tcc * 128: dc * T + (tcc + 1) * 128],
                        whead[2][:, dc * 128:(dc + 1) * 128],
                        start=(dc == 0), stop=(dc == 15))
                    if dc == 15:
                        cx.pe.inc(m, 1)
                cx.pe.wait(nc.vector)
                V(cx, nc.vector.tensor_copy, vt[:, tcc * 128:(tcc + 1) * 128], ps_v[:, :])

            V(cx, nc.vector.memset, kv_f[1][:, :], 0.0)
            V(cx, nc.vector.memset, kv_b[:, :], 0.0)

            for ci in range(NCHUNK):
                c0 = ci * 128
                kcur = kv_f[ci % 2]
                kprev = kv_f[(ci + 1) % 2]
                cx.dve.wait(nc.tensor)
                cx.pe.inc(nc.tensor.matmul(ps_sc[:, :], kf[:, c0:c0 + 128],
                                           qf[:, c0:c0 + 128], start=True, stop=True), 1)
                cx.pe.wait(nc.vector)
                if ci == 0:
                    c_dec.wait(nc.vector)
                V(cx, nc.vector.tensor_tensor, sT_t[:, :], ps_sc[:, :],
                  dmask_h[:, :], ALU.mult)
                V(cx, nc.vector.tensor_tensor, qd_t[:, :], qf[:, c0:c0 + 128],
                  qdec_h[:, :], ALU.mult)
                cx.dve.wait(nc.tensor)
                if ci > 0:
                    cx.pe.inc(nc.tensor.matmul(ps_at[:, :], qd_t[:, :], kv_b[:, :],
                                               start=True, stop=False), 1)
                cx.pe.inc(nc.tensor.matmul(ps_at[:, :], sT_t[:, :],
                                           vt[:, c0:c0 + 128],
                                           start=(ci == 0), stop=True), 1)
                cx.pe.wait(nc.vector)
                V(cx, nc.vector.tensor_copy,
                  attn_tm[:, ci * D + h * 128: ci * D + (h + 1) * 128], ps_at[:, :])
                V(cx, nc.vector.tensor_tensor, kd_t[:, :], kf[:, c0:c0 + 128],
                  kdec_h[:, :], ALU.mult)
                cx.dve.wait(nc.tensor)
                cx.pe.inc(nc.tensor.transpose(ps_tr[:, :], kd_t[:, :],
                                              ident_bf16[:, :]), 1)
                cx.pe.wait(nc.vector)
                V(cx, nc.vector.tensor_copy, kdtm[:, :], ps_tr[:, :])
                cx.dve.wait(nc.tensor)
                cx.pe.inc(nc.tensor.matmul(ps_kv[:, :], kdtm[:, :],
                                           vt[:, c0:c0 + 128], start=True, stop=True), 1)
                cx.pe.wait(nc.vector)
                V(cx, nc.vector.scalar_tensor_tensor, kcur[:, :], kprev[:, :],
                  r_C[h], ps_kv[:, :], ALU.mult, ALU.add)
                V(cx, nc.vector.tensor_copy, kv_b[:, :], kcur[:, :])

        for tcc in range(nTC):
            att_sl = attn_tm[:, tcc * D:(tcc + 1) * D]
            emit_ln_stats(cx, att_sl, D, stp, sq, rms_only=True)
            V(cx, nc.vector.scalar_tensor_tensor, xn_tm[:, :], att_sl,
              stp["rstd"][:, 0:1], rmsw_b[:, :], ALU.mult, ALU.mult)
            for ec in range(16):
                cx.dve.wait(nc.tensor)
                cx.pe.inc(nc.tensor.transpose(
                    ps_tr[:, :], xn_tm[:, ec * 128:(ec + 1) * 128], ident_bf16[:, :]), 1)
                cx.pe.wait(nc.vector)
                V(cx, nc.vector.tensor_copy,
                  big_fm[:, ec * T + tcc * 128: ec * T + (tcc + 1) * 128], ps_tr[:, :])

        for do in range(4):
            cx.pe.wait(nc.sync)
            dma_in(cx, c_wo, wo_sl[:, :].rearrange("p (c f) -> p c f", c=16),
                   w["WoB"][:, do * 512:(do + 1) * 512]
                   .rearrange("(c p) f -> c p f", p=128).transpose([1, 0, 2]))
            c_wo.wait(nc.tensor)
            for tcc in range(nTC):
                row0 = base + tcc * 128
                k = tcc % 2
                cx.dve.wait(nc.sync)
                dma_in(cx, c_res[k], res[k][:, :],
                       x_in[row0:row0 + 128, do * 512:(do + 1) * 512])
                cx.dve.wait(nc.tensor)
                for ec in range(16):
                    m = nc.tensor.matmul(
                        ps_p[:, :], big_fm[:, ec * T + tcc * 128: ec * T + (tcc + 1) * 128],
                        wo_sl[:, ec * 512:(ec + 1) * 512],
                        start=(ec == 0), stop=(ec == 15))
                    if ec == 15:
                        cx.pe.inc(m, 1)
                cx.pe.wait(nc.vector)
                c_res[k].wait(nc.vector)
                c_out[k].wait(nc.vector)
                V(cx, nc.vector.tensor_tensor, out_t[k][:, :], ps_p[:, :],
                  res[k][:, :], ALU.add)
                cx.dve.wait(nc.gpsimd)
                dma_out(cx, c_out[k],
                        x_out[row0:row0 + 128, do * 512:(do + 1) * 512], out_t[k][:, :])


def build_attn_program():
    nc = bass.Bass("TRN2", target_bir_lowering=False)
    x_in = nc.dram_tensor("x_in", [TOK, D], F32, kind="ExternalInput").ap()
    x_out = nc.dram_tensor("x_out", [TOK, D], F32, kind="ExternalOutput").ap()
    ident = nc.dram_tensor("ident", [128, 128], F32, kind="ExternalInput").ap()
    identb = nc.dram_tensor("identb", [128, 128], BF16, kind="ExternalInput").ap()
    w = declare_attn_io(nc)
    hc = attn_host_constants()
    with ExitStack() as stack:
        cx = Ctx(nc, stack)
        cx.init_consts()
        ident_sb = cx.sb("ident_sb", [128, 128], F32)
        identb_sb = cx.sb("identb_sb", [128, 128], BF16)
        c0 = cx.dma_ctr("ident")
        dma_in(cx, c0, ident_sb[:, :], ident[:, :])
        dma_in(cx, c0, identb_sb[:, :], identb[:, :])
        c0.wait(nc.tensor)
        emit_attn_segment(cx, x_in, x_out, w, hc["r_C"], identb_sb)
        cx.final_barrier()
    return nc





STATE = 64
DT_RANK = 128
SSM_CHUNK = 512


def scan_host_constants():
    bf = ml_dtypes.bfloat16
    c = {}
    Eg = np.zeros((8, 128, 128), dtype=np.float32)
    for gp in range(8):
        for p in range(128):
            Eg[gp, 16 * gp + p // 8, p] = 1.0
    c["Eg"] = Eg.transpose(1, 0, 2).reshape(128, 8 * 128).astype(bf)
    EB = np.zeros((8, 128, 128), dtype=np.float32)
    EC = np.zeros((8, 128, 128), dtype=np.float32)
    for j in range(8):
        for p in range(128):
            EB[j, 8 * j + p % 8, p] = 1.0
            EC[j, 64 + 8 * j + p % 8, p] = 1.0
    c["EB"] = EB.transpose(1, 0, 2).reshape(128, 8 * 128).astype(bf)
    c["EC"] = EC.transpose(1, 0, 2).reshape(128, 8 * 128).astype(bf)
    Er = np.zeros((8, 128, 128), dtype=np.float32)
    for gs in range(8):
        for m in range(16 * gs, 16 * gs + 16):
            for p in range(128):
                if p // 8 == m % 16:
                    Er[gs, p, m] = 1.0
    c["Ered"] = Er.transpose(1, 0, 2).reshape(128, 8 * 128).astype(bf)
    return c


def rearrange_A_log(A_log):
    return np.ascontiguousarray(
        A_log.reshape(128, 16, 8, 8).transpose(1, 3, 0, 2).reshape(128, 1024))


def declare_scan_io(nc, prefix=""):
    q = prefix
    d = {
        "norm_w": nc.dram_tensor(q + "norm_w", [D], F32, kind="ExternalInput").ap(),
        "norm_b": nc.dram_tensor(q + "norm_b", [D], F32, kind="ExternalInput").ap(),
        "x_proj_w": nc.dram_tensor(q + "x_proj_w", [256, D], F32, kind="ExternalInput").ap(),
        "dt_proj_w": nc.dram_tensor(q + "dt_proj_w", [D, DT_RANK], F32, kind="ExternalInput").ap(),
        "dt_proj_b": nc.dram_tensor(q + "dt_proj_b", [D], F32, kind="ExternalInput").ap(),
        "A_log_sc": nc.dram_tensor(q + "A_log_sc", [128, 1024], F32, kind="ExternalInput").ap(),
        "D_param": nc.dram_tensor(q + "D_param", [D], F32, kind="ExternalInput").ap(),
        "out_proj_w": nc.dram_tensor(q + "out_proj_w", [D, D], F32, kind="ExternalInput").ap(),
        "Eg": nc.dram_tensor(q + "Eg", [128, 8 * 128], BF16, kind="ExternalInput").ap(),
        "EB": nc.dram_tensor(q + "EB", [128, 8 * 128], BF16, kind="ExternalInput").ap(),
        "EC": nc.dram_tensor(q + "EC", [128, 8 * 128], BF16, kind="ExternalInput").ap(),
        "Ered": nc.dram_tensor(q + "Ered", [128, 8 * 128], BF16, kind="ExternalInput").ap(),
        "xpwT": nc.dram_tensor(q + "xpwT", [16, 2, 128, 128], BF16).ap(),
        "dtpwT": nc.dram_tensor(q + "dtpwT", [1, 16, 128, 128], BF16).ap(),
        "opwT": nc.dram_tensor(q + "opwT", [16, 16, 128, 128], BF16).ap(),
        "xn2b_d": nc.dram_tensor(q + "xn2b_d", [16, 128, T], BF16).ap(),
        "x1d_d": nc.dram_tensor(q + "x1d_d", [16, 128, T], F32).ap(),
        "dthi_d": nc.dram_tensor(q + "dthi_d", [16, 128, T], BF16).ap(),
        "dtlo_d": nc.dram_tensor(q + "dtlo_d", [16, 128, T], BF16).ap(),
        "dtx_d": nc.dram_tensor(q + "dtx_d", [16, 128, T], BF16).ap(),
        "brep_d": nc.dram_tensor(q + "brep_d", [8, 128, T], BF16).ap(),
        "crep_d": nc.dram_tensor(q + "crep_d", [8, 128, T], BF16).ap(),
        "pre_d": nc.dram_tensor(q + "pre_d", [16, 128, T], BF16).ap(),
    }
    return d


def emit_scan_segment(cx, x_in, x_out, w, ident_f32, ident_bf16):
    nc = cx.nc
    nTC = T // 128
    nSL = T // 512

    emit_prep_transpose(cx, w["x_proj_w"], w["xpwT"], 256, D, ident_f32)
    emit_prep_transpose(cx, w["dt_proj_w"], w["dtpwT"], D, DT_RANK, ident_f32)
    emit_prep_transpose(cx, w["out_proj_w"], w["opwT"], D, D, ident_f32)

    c_const = cx.dma_ctr("sc_const")
    c_cast = cx.dma_ctr("sc_cast")
    Eg = cx.sb("sc_Eg", [128, 8 * 128], BF16)
    EB = cx.sb("sc_EB", [128, 8 * 128], BF16)
    EC = cx.sb("sc_EC", [128, 8 * 128], BF16)
    Ered = cx.sb("sc_Ered", [128, 8 * 128], BF16)
    dma_in(cx, c_const, Eg[:, :], w["Eg"][:, :])
    dma_in(cx, c_const, EB[:, :], w["EB"][:, :])
    dma_in(cx, c_const, EC[:, :], w["EC"][:, :])
    dma_in(cx, c_const, Ered[:, :], w["Ered"][:, :])
    nw_b = cx.sb("sc_nw", [128, D], BF16)
    nb_b = cx.sb("sc_nb", [128, D], BF16)
    dp_b = cx.sb("sc_dp", [128, D], F32)
    dma_in(cx, c_cast, nw_b[0:128, 0:D],
           w["norm_w"][0:D].unsqueeze(0).partition_broadcast(128), eng=nc.gpsimd)
    dma_in(cx, c_cast, nb_b[0:128, 0:D],
           w["norm_b"][0:D].unsqueeze(0).partition_broadcast(128), eng=nc.gpsimd)
    bcast_row(cx, c_const, dp_b, w["D_param"], D)
    dtb_cols = cx.sb("sc_dtb", [128, 16], F32)
    for dc in range(16):
        load_col(cx, c_const, dtb_cols[:, dc:dc + 1], w["dt_proj_b"], dc * 128)
    alog = cx.sb("sc_alog", [128, 1024], F32)
    a_neg = cx.sb("sc_aneg", [128, 1024], F32)
    dma_in(cx, c_const, alog[:, :], w["A_log_sc"][:, :])
    c_const.wait(nc.scalar)
    c_const.wait(nc.vector)
    c_const.wait(nc.tensor)
    c_cast.wait(nc.vector)
    A(cx, nc.scalar.activation, a_neg[:, :], alog[:, :], AF.Exp, bias=0.0, scale=1.0)
    cx.act.wait(nc.vector)
    V(cx, nc.vector.tensor_scalar, a_neg[:, :], a_neg[:, :], -1.0, None, ALU.mult)

    xpw_sb = cx.sb("sc_xpw", [128, 16 * 256], BF16)
    dtpw_sb = cx.sb("sc_dtpw", [128, 16 * 128], BF16)
    c_wsl = cx.dma_ctr("sc_wsl")
    for r_ in range(2):
        dma_in(cx, c_wsl,
               xpw_sb[:, :].rearrange("p (c r f) -> p c r f", c=16, r=2)[:, :, r_, :],
               w["xpwT"][:, r_].transpose([1, 0, 2]))
    dma_in(cx, c_wsl, dtpw_sb[:, :].rearrange("p (c f) -> p c f", c=16),
           w["dtpwT"][0].transpose([1, 0, 2]))
    c_wsl.wait(nc.tensor)

    stp = mk_stat_pool(cx, "sc")
    xt = cx.sb("sc_xt", [128, D], F32)
    c_xt = cx.dma_ctr("sc_xt")
    sq = cx.sb("sc_sq", [128, D], F32)
    xn1 = cx.sb("sc_xn1", [128, D], F32)
    xn2b = cx.sb("sc_xn2b", [128, D], BF16)
    x1d_t = cx.sb("sc_x1d", [128, D], F32)
    delta_fm = cx.sb("sc_delta", [128, T], BF16)
    bc_fm = cx.sb("sc_bc", [128, T], BF16)
    projtm = cx.sb("sc_projtm", [128, 256], BF16)
    xn2sl = [cx.sb(f"sc_xn2sl{i}", [128, 16 * 128], BF16) for i in range(2)]
    c_xn2sl = [cx.dma_ctr("sc_xn2sl0"), cx.dma_ctr("sc_xn2sl1")]
    wt1 = cx.sb("sc_wt1", [128, 512], F32)
    wt2 = cx.sb("sc_wt2", [128, 512], F32)
    wt3 = cx.sb("sc_wt3", [128, 512], F32)
    dt_f = cx.sb("sc_dtf", [128, 512], F32)
    dthl = cx.sb("sc_dthl", [128, 512], BF16)
    dtx_t = cx.sb("sc_dtxt", [128, 512], BF16)
    xn2ld = cx.sb("sc_xn2ld", [128, 512], BF16)
    c_xn2ld = cx.dma_ctr("sc_xn2ld")
    dt3 = [cx.sb("sc_dt3_0", [128, 3 * T], BF16)]
    c_dt3 = [cx.dma_ctr("sc_dt30")]
    dtrep = cx.sb("sc_dtrep", [128, T], F32)
    dtxrep = cx.sb("sc_dtxrep", [128, T], BF16)
    dA_sb = cx.sb("sc_dA", [128, T], F32)
    u_sb = cx.sb("sc_u", [128, T], BF16)
    h_sb = cx.sb("sc_h", [128, T], BF16)
    g_sb = cx.sb("sc_g", [128, T], BF16)
    brt = [cx.sb(f"sc_brt{i}", [128, T], BF16) for i in range(2)]
    crt = [cx.sb(f"sc_crt{i}", [128, T], BF16) for i in range(2)]
    c_br = [cx.dma_ctr("sc_br0"), cx.dma_ctr("sc_br1")]
    x1dt = [cx.sb(f"sc_x1dt{i}", [128, 512], F32) for i in range(2)]
    c_x1dt = [cx.dma_ctr("sc_x1dt0"), cx.dma_ctr("sc_x1dt1")]
    pre_t = cx.sb("sc_pret", [128, 512], BF16)
    wslab = cx.sb("sc_wslab", [128, 16 * 512], BF16)
    c_wo = cx.dma_ctr("sc_wo")
    preslab = [cx.sb(f"sc_presl{i}", [128, 16 * 128], BF16) for i in range(2)]
    c_pre = [cx.dma_ctr("sc_pre0"), cx.dma_ctr("sc_pre1")]
    res = [cx.sb(f"sc_res{i}", [128, 512], F32) for i in range(2)]
    c_res = [cx.dma_ctr("sc_res0"), cx.dma_ctr("sc_res1")]
    out_t = [cx.sb(f"sc_out{i}", [128, 512], F32) for i in range(2)]
    c_out = [cx.dma_ctr("sc_out0", store=True), cx.dma_ctr("sc_out1", store=True)]
    c_st = cx.dma_ctr("sc_store", store=True)

    ps_tr = [cx.ps(f"sc_pstr{i}", [128, 128], BF16) for i in range(2)]
    ps_trf = cx.ps("sc_pstrf", [128, 128], F32)
    ps_pj = cx.ps("sc_pspj", [128, 512])
    ps_y = [cx.ps(f"sc_psy{i}", [128, 512]) for i in range(4)]

    for b in range(B_PER_CORE):
        base = b * T
        for ts in range(nTC):
            row0 = base + ts * 128
            cx.dve.wait(nc.sync)
            cx.act.wait(nc.sync)
            dma_in(cx, c_xt, xt[:, :], x_in[row0:row0 + 128, :])
            c_xt.wait(nc.vector)
            c_xt.wait(nc.scalar)
            emit_ln_stats(cx, xt, D, stp, sq, out_f32=xn1)
            V(cx, nc.vector.tensor_tensor, x1d_t[:, :], xn1[:, :], dp_b[:, :], ALU.mult)
            for dc in range(16):
                cx.dve.wait(nc.tensor)
                cx.pe.inc(nc.tensor.transpose(
                    ps_trf[:, :], x1d_t[:, dc * 128:(dc + 1) * 128], ident_f32[:, :]), 1)
                cx.pe.wait(nc.vector)
                c_st.wait(nc.vector)
                V(cx, nc.vector.tensor_copy, wt1[:, 0:128], ps_trf[:, :])
                cx.dve.wait(nc.gpsimd)
                dma_out(cx, c_st, w["x1d_d"][dc, :, ts * 128:(ts + 1) * 128],
                        wt1[:, 0:128])
            emit_ln_stats(cx, xn1, D, stp, sq, out_f32=xt)
            V(cx, nc.vector.tensor_tensor, xt[:, :], xt[:, :], nw_b[:, :], ALU.mult)
            V(cx, nc.vector.tensor_tensor, xn2b[:, :], xt[:, :], nb_b[:, :], ALU.add)
            for dc in range(16):
                cx.dve.wait(nc.tensor)
                cx.pe.inc(nc.tensor.transpose(
                    ps_tr[dc % 2][:, :], xn2b[:, dc * 128:(dc + 1) * 128],
                    ident_bf16[:, :]), 1)
                cx.pe.wait(nc.vector)
                c_st.wait(nc.vector)
                V(cx, nc.vector.tensor_copy, dthl[:, 0:128], ps_tr[dc % 2][:, :])
                cx.dve.wait(nc.gpsimd)
                dma_out(cx, c_st, w["xn2b_d"][dc, :, ts * 128:(ts + 1) * 128],
                        dthl[:, 0:128])
        c_st.wait(nc.sync)

        for ts in range(nTC):
            k = ts % 2
            cx.pe.wait(nc.sync)
            dma_in(cx, c_xn2sl[k], xn2sl[k][:, :].rearrange("p (c t) -> p c t", c=16),
                   w["xn2b_d"][:, :, ts * 128:(ts + 1) * 128].transpose([1, 0, 2]))
            c_xn2sl[k].wait(nc.tensor)
            cx.dve.wait(nc.tensor)
            for dc in range(16):
                m = nc.tensor.matmul(ps_pj[:, 0:256],
                                     xn2sl[k][:, dc * 128:(dc + 1) * 128],
                                     xpw_sb[:, dc * 256:(dc + 1) * 256],
                                     start=(dc == 0), stop=(dc == 15))
                if dc == 15:
                    cx.pe.inc(m, 1)
            cx.pe.wait(nc.vector)
            V(cx, nc.vector.tensor_copy, projtm[:, :], ps_pj[:, 0:256])
            cx.dve.wait(nc.tensor)
            cx.pe.inc(nc.tensor.transpose(ps_tr[0][:, :], projtm[:, 0:128],
                                          ident_bf16[:, :]), 1)
            cx.pe.inc(nc.tensor.transpose(ps_tr[1][:, :], projtm[:, 128:256],
                                          ident_bf16[:, :]), 1)
            cx.pe.wait(nc.vector)
            V(cx, nc.vector.tensor_copy, delta_fm[:, ts * 128:(ts + 1) * 128],
              ps_tr[0][:, :])
            V(cx, nc.vector.tensor_copy, bc_fm[:, ts * 128:(ts + 1) * 128],
              ps_tr[1][:, :])

        for j in range(8):
            for which, sel, dst_d in ((0, EB, "brep_d"), (1, EC, "crep_d")):
                for sl in range(nSL):
                    rhs = bc_fm[:, sl * 512:(sl + 1) * 512] if which == 0                         else bc_fm[:, 0:512]
                    cx.dve.wait(nc.tensor)
                    cx.pe.inc(nc.tensor.matmul(
                        ps_pj[:, :], sel[:, j * 128:(j + 1) * 128], rhs,
                        start=True, stop=True), 1)
                    cx.pe.wait(nc.vector)
                    c_st.wait(nc.vector)
                    V(cx, nc.vector.tensor_copy, dthl[:, :], ps_pj[:, :])
                    cx.dve.wait(nc.gpsimd)
                    dma_out(cx, c_st, w[dst_d][j, :, sl * 512:(sl + 1) * 512],
                            dthl[:, :])
        c_st.wait(nc.sync)

        for dc in range(16):
            for sl in range(nSL):
                cx.dve.wait(nc.sync)
                dma_in(cx, c_xn2ld, xn2ld[:, :],
                       w["xn2b_d"][dc, :, sl * 512:(sl + 1) * 512])
                cx.act.wait(nc.tensor)
                cx.dve.wait(nc.tensor)
                cx.pe.inc(nc.tensor.matmul(
                    ps_pj[:, :], dtpw_sb[:, dc * 128:(dc + 1) * 128],
                    delta_fm[:, sl * 512:(sl + 1) * 512],
                    start=True, stop=True), 1)
                cx.pe.wait(nc.scalar)
                A(cx, nc.scalar.activation, wt1[:, :], ps_pj[:, :], AF.Identity,
                  bias=dtb_cols[:, dc:dc + 1], scale=1.0)
                A(cx, nc.scalar.activation, wt2[:, :], wt1[:, :], AF.Abs,
                  bias=0.0, scale=1.0)
                A(cx, nc.scalar.activation, wt3[:, :], wt2[:, :], AF.Exp,
                  bias=0.0, scale=-1.0)
                A(cx, nc.scalar.activation, wt2[:, :], wt3[:, :], AF.Ln,
                  bias=1.0, scale=1.0)
                A(cx, nc.scalar.activation, wt3[:, :], wt1[:, :], AF.Relu,
                  bias=0.0, scale=1.0)
                cx.act.wait(nc.vector)
                c_st.wait(nc.vector)
                V(cx, nc.vector.tensor_add, dt_f[:, :], wt2[:, :], wt3[:, :])
                V(cx, nc.vector.tensor_copy, dthl[:, :], dt_f[:, :])
                cx.dve.wait(nc.gpsimd)
                dma_out(cx, c_st, w["dthi_d"][dc, :, sl * 512:(sl + 1) * 512],
                        dthl[:, :])
                c_st.wait(nc.vector)
                V(cx, nc.vector.tensor_tensor, dthl[:, :], dt_f[:, :], dthl[:, :],
                  ALU.subtract)
                cx.dve.wait(nc.gpsimd)
                dma_out(cx, c_st, w["dtlo_d"][dc, :, sl * 512:(sl + 1) * 512],
                        dthl[:, :])
                c_xn2ld.wait(nc.vector)
                V(cx, nc.vector.tensor_tensor, dtx_t[:, :], dt_f[:, :],
                  xn2ld[:, :], ALU.mult)
                cx.dve.wait(nc.gpsimd)
                dma_out(cx, c_st, w["dtx_d"][dc, :, sl * 512:(sl + 1) * 512],
                        dtx_t[:, :])
        c_st.wait(nc.sync)

        for dc in range(16):
            k = 0
            cx.pe.wait(nc.sync)
            dma_in(cx, c_dt3[k], dt3[k][:, 0:T], w["dthi_d"][dc])
            dma_in(cx, c_dt3[k], dt3[k][:, T:2 * T], w["dtlo_d"][dc])
            dma_in(cx, c_dt3[k], dt3[k][:, 2 * T:3 * T], w["dtx_d"][dc])
            c_dt3[k].wait(nc.tensor)
            for gp in range(8):
                g = dc * 8 + gp
                for sl in range(nSL):
                    cx.act.wait(nc.tensor)
                    m = nc.tensor.matmul(ps_pj[:, :], Eg[:, gp * 128:(gp + 1) * 128],
                                         dt3[k][:, sl * 512:(sl + 1) * 512],
                                         start=True, stop=False)
                    m = nc.tensor.matmul(ps_pj[:, :], Eg[:, gp * 128:(gp + 1) * 128],
                                         dt3[k][:, T + sl * 512:T + (sl + 1) * 512],
                                         start=False, stop=True)
                    cx.pe.inc(m, 1)
                    cx.pe.wait(nc.scalar)
                    cx.dve.wait(nc.scalar)
                    A(cx, nc.scalar.activation, dtrep[:, sl * 512:(sl + 1) * 512],
                      ps_pj[:, :], AF.Copy, bias=0.0, scale=1.0)
                    cx.act.wait(nc.tensor)
                    m = nc.tensor.matmul(ps_pj[:, :], Eg[:, gp * 128:(gp + 1) * 128],
                                         dt3[k][:, 2 * T + sl * 512:2 * T + (sl + 1) * 512],
                                         start=True, stop=True)
                    cx.pe.inc(m, 1)
                    cx.pe.wait(nc.scalar)
                    cx.dve.wait(nc.scalar)
                    A(cx, nc.scalar.activation, dtxrep[:, sl * 512:(sl + 1) * 512],
                      ps_pj[:, :], AF.Copy, bias=0.0, scale=1.0)
                for j in range(8):
                    kb = j % 2
                    cx.dve.wait(nc.sync)
                    dma_in(cx, c_br[kb], brt[kb][:, :], w["brep_d"][j])
                    dma_in(cx, c_br[kb], crt[kb][:, :], w["crep_d"][j])
                    cx.act.wait(nc.vector)
                    A_col = a_neg[:, g * 8 + j: g * 8 + j + 1]
                    cx.dve.wait(nc.scalar)
                    A(cx, nc.scalar.activation, dA_sb[:, :], dtrep[:, :], AF.Exp,
                      bias=0.0, scale=A_col)
                    c_br[kb].wait(nc.vector)
                    V(cx, nc.vector.tensor_tensor, u_sb[:, :], dtxrep[:, :],
                      brt[kb][:, :], ALU.mult)
                    cx.act.wait(nc.vector)
                    V(cx, nc.vector.tensor_tensor_scan, h_sb[:, :], dA_sb[:, :],
                      u_sb[:, :], 0.0, ALU.mult, ALU.add)
                    cx.pe.wait(nc.vector)
                    V(cx, nc.vector.tensor_tensor, g_sb[:, :], h_sb[:, :],
                      crt[kb][:, :], ALU.mult)
                    cx.dve.wait(nc.tensor)
                    for sl in range(nSL):
                        m = nc.tensor.matmul(
                            ps_y[sl][:, :], Ered[:, gp * 128:(gp + 1) * 128],
                            g_sb[:, sl * 512:(sl + 1) * 512],
                            start=(gp == 0 and j == 0), stop=(gp == 7 and j == 7))
                        if sl == nSL - 1:
                            cx.pe.inc(m, 1)
            cx.pe.wait(nc.vector)
            for sl in range(nSL):
                kx = sl % 2
                cx.dve.wait(nc.sync)
                dma_in(cx, c_x1dt[kx], x1dt[kx][:, :],
                       w["x1d_d"][dc, :, sl * 512:(sl + 1) * 512])
                c_x1dt[kx].wait(nc.vector)
                c_st.wait(nc.vector)
                V(cx, nc.vector.tensor_tensor, pre_t[:, :], ps_y[sl][:, :],
                  x1dt[kx][:, :], ALU.add)
                cx.dve.wait(nc.gpsimd)
                dma_out(cx, c_st, w["pre_d"][dc, :, sl * 512:(sl + 1) * 512],
                        pre_t[:, :])
        c_st.wait(nc.sync)

        for eo in range(4):
            cx.pe.wait(nc.sync)
            for o in range(4):
                dma_in(cx, c_wo,
                       wslab[:, :].rearrange("p (c o f) -> p c o f", c=16, o=4)[:, :, o, :],
                       w["opwT"][:, eo * 4 + o].transpose([1, 0, 2]))
            c_wo.wait(nc.tensor)
            for tc in range(nTC):
                row0 = base + tc * 128
                kx = tc % 2
                cx.pe.wait(nc.sync)
                dma_in(cx, c_pre[kx], preslab[kx][:, :].rearrange("p (c t) -> p c t", c=16),
                       w["pre_d"][:, :, tc * 128:(tc + 1) * 128].transpose([1, 0, 2]))
                cx.dve.wait(nc.sync)
                dma_in(cx, c_res[kx], res[kx][:, :],
                       x_in[row0:row0 + 128, eo * 512:(eo + 1) * 512])
                c_pre[kx].wait(nc.tensor)
                cx.dve.wait(nc.tensor)
                for dc in range(16):
                    m = nc.tensor.matmul(
                        ps_pj[:, :], preslab[kx][:, dc * 128:(dc + 1) * 128],
                        wslab[:, dc * 512:(dc + 1) * 512],
                        start=(dc == 0), stop=(dc == 15))
                    if dc == 15:
                        cx.pe.inc(m, 1)
                cx.pe.wait(nc.vector)
                c_res[kx].wait(nc.vector)
                c_out[kx].wait(nc.vector)
                V(cx, nc.vector.tensor_tensor, out_t[kx][:, :], ps_pj[:, :],
                  res[kx][:, :], ALU.add)
                cx.dve.wait(nc.gpsimd)
                dma_out(cx, c_out[kx],
                        x_out[row0:row0 + 128, eo * 512:(eo + 1) * 512],
                        out_t[kx][:, :])


def build_scan_program():
    nc = bass.Bass("TRN2", target_bir_lowering=False)
    x_in = nc.dram_tensor("x_in", [TOK, D], F32, kind="ExternalInput").ap()
    x_out = nc.dram_tensor("x_out", [TOK, D], F32, kind="ExternalOutput").ap()
    ident = nc.dram_tensor("ident", [128, 128], F32, kind="ExternalInput").ap()
    identb = nc.dram_tensor("identb", [128, 128], BF16, kind="ExternalInput").ap()
    w = declare_scan_io(nc)
    with ExitStack() as stack:
        cx = Ctx(nc, stack)
        cx.init_consts()
        ident_sb = cx.sb("ident_sb", [128, 128], F32)
        identb_sb = cx.sb("identb_sb", [128, 128], BF16)
        c0 = cx.dma_ctr("ident")
        dma_in(cx, c0, ident_sb[:, :], ident[:, :])
        dma_in(cx, c0, identb_sb[:, :], identb[:, :])
        c0.wait(nc.tensor)
        emit_scan_segment(cx, x_in, x_out, w, ident_sb, identb_sb)
        cx.final_barrier()
    return nc



_PROGS = {}


def _get_prog(name, builder):
    if name not in _PROGS:
        _PROGS[name] = builder()
    return _PROGS[name]


def kernel(**inputs):
    x = np.ascontiguousarray(inputs["L0_attn_x"], dtype=np.float32)
    shards = [x[2 * c:2 * c + 2].reshape(TOK, D) for c in range(NCORES)]

    hc = host_constants()
    ac = attn_host_constants()
    sc = scan_host_constants()
    base_c = {"ident": hc["ident"], "identb": hc["identb"]}

    nc_a = _get_prog("attn", build_attn_program)
    com_a = dict(base_c,
                 Wq=inputs["L0_attn_Wq"], Wk=inputs["L0_attn_Wk"],
                 Wv=inputs["L0_attn_Wv"], Wo=inputs["L0_attn_Wo"],
                 rms_w=inputs["L0_attn_rms_w"],
                 qdecT=np.asarray(ac["qdecT"]), kdecT=np.asarray(ac["kdecT"]),
                 dmaskT=np.asarray(ac["dmaskT"]), cosT=np.asarray(ac["cosT"]),
                 sinS=np.asarray(ac["sinS"]), Pswap=np.asarray(ac["Pswap"]))
    res = run_bass_kernel_spmd(
        nc_a, [dict(com_a, x_in=s_) for s_ in shards], core_ids=list(range(NCORES)))
    shards = [r["x_out"] for r in res.results]

    nc_f0 = _get_prog("ffn0", lambda: build_ffn_program(0))
    com_f0 = dict(base_c,
                  gate_w=inputs["L0_ffn_gate_w"], up_w=inputs["L0_ffn_up_w"],
                  down_w=inputs["L0_ffn_down_w"], gate_s=inputs["L0_ffn_gate_s"],
                  up_s=inputs["L0_ffn_up_s"], down_s=inputs["L0_ffn_down_s"])
    res = run_bass_kernel_spmd(
        nc_f0, [dict(com_f0, x_in=s_) for s_ in shards], core_ids=list(range(NCORES)))
    shards = [r["x_out"] for r in res.results]

    nc_s = _get_prog("scan", build_scan_program)
    com_s = dict(base_c,
                 norm_w=inputs["L1_attn_norm_w"], norm_b=inputs["L1_attn_norm_b"],
                 x_proj_w=inputs["L1_attn_x_proj_w"],
                 dt_proj_w=inputs["L1_attn_dt_proj_w"],
                 dt_proj_b=inputs["L1_attn_dt_proj_b"],
                 A_log_sc=rearrange_A_log(np.asarray(inputs["L1_attn_A_log"])),
                 D_param=inputs["L1_attn_D_param"],
                 out_proj_w=inputs["L1_attn_out_proj_w"],
                 Eg=np.asarray(sc["Eg"]), EB=np.asarray(sc["EB"]),
                 EC=np.asarray(sc["EC"]), Ered=np.asarray(sc["Ered"]))
    res = run_bass_kernel_spmd(
        nc_s, [dict(com_s, x_in=s_) for s_ in shards], core_ids=list(range(NCORES)))
    shards = [r["x_out"] for r in res.results]

    nc_f1 = _get_prog("ffn1", lambda: build_ffn_program(1))
    com_f1 = dict(base_c,
                  up_w=inputs["L1_ffn_up_weight"], down_w=inputs["L1_ffn_down_weight"],
                  up_s=inputs["L1_ffn_up_scale"], down_s=inputs["L1_ffn_down_scale"],
                  out_s=inputs["L1_ffn_output_scale"])
    res = run_bass_kernel_spmd(
        nc_f1, [dict(com_f1, x_in=s_) for s_ in shards], core_ids=list(range(NCORES)))
    shards = [r["x_out"] for r in res.results]

    out = np.stack([s_.reshape(2, T, D) for s_ in shards], 0).reshape(16, T, D)
    return out


# version 05 (final)
# speedup vs baseline: 1.2747x; 1.2747x over previous
import numpy as np
import ml_dtypes
import concourse.bass as bass
import concourse.mybir as mybir
from concourse.bass_utils import run_bass_kernel_spmd
from contextlib import ExitStack

F32 = mybir.dt.float32
BF16 = mybir.dt.bfloat16
AF = mybir.ActivationFunctionType
ALU = mybir.AluOpType
AX = mybir.AxisListType

NCORES = 8
B_PER_CORE = 2
T = 2048
D = 2048
TOK = B_PER_CORE * T
EPS = 1e-5

_UID = [0]

SELF_WAITS = True
SELF_WAITS_V = True
SELF_WAITS_A = False

def uname(base):
    _UID[0] += 1
    return f"{base}_{_UID[0]}"


class Ctr:
    def __init__(self, nc, name, stack):
        self.sem = stack.enter_context(nc.semaphore(uname(name)))
        self.val = 0

    def inc(self, inst, n=1):
        inst.then_inc(self.sem, n)
        self.val += n

    def wait(self, eng, val=None):
        eng.wait_ge(self.sem, self.val if val is None else val)


class Ctx:
    def __init__(self, nc, stack, sem_stack=None):
        self.nc = nc
        self.stack = stack
        self.sem_stack = sem_stack or stack
        self.pe = Ctr(nc, "c_pe", self.sem_stack)
        self.act = Ctr(nc, "c_act", self.sem_stack)
        self.dve = Ctr(nc, "c_dve", self.sem_stack)
        self.pool = Ctr(nc, "c_pool", self.sem_stack)
        self.store_ctrs = []
        self.all_dma_ctrs = []
        self._ctr_cache = {}

    def dma_ctr(self, name, store=False):
        if name in self._ctr_cache:
            return self._ctr_cache[name]
        c = Ctr(self.nc, "d_" + name, self.sem_stack)
        if store:
            self.store_ctrs.append(c)
        self.all_dma_ctrs.append(c)
        self._ctr_cache[name] = c
        return c

    def segment_barrier(self):
        nc = self.nc
        engines = (nc.sync, nc.gpsimd, nc.scalar, nc.vector, nc.tensor)
        for eng in engines:
            for c in self.all_dma_ctrs:
                c.wait(eng)
            self.pe.wait(eng)
            self.act.wait(eng)
            self.dve.wait(eng)
            self.pool.wait(eng)

    def init_consts(self):
        self.eps_col = self.sb("eps_col", [128, 1], F32)
        self.dve.inc(self.nc.vector.memset(self.eps_col[:, 0:1], EPS), 1)

    def sb(self, name, shape, dtype):
        return self.stack.enter_context(self.nc.sbuf_tensor(uname(name), shape, dtype))

    def ps(self, name, shape, dtype=F32):
        return self.stack.enter_context(self.nc.psum_tensor(uname(name), shape, dtype))

    def final_barrier(self):
        for c in self.all_dma_ctrs:
            c.wait(self.nc.gpsimd)


def V(cx, f, *a, **kw):
    if SELF_WAITS and SELF_WAITS_V:
        cx.dve.wait(cx.nc.vector)
    cx.dve.inc(f(*a, **kw), 1)


def A(cx, f, *a, **kw):
    if SELF_WAITS_A or (SELF_WAITS and False):
        cx.act.wait(cx.nc.scalar)
    cx.act.inc(f(*a, **kw), 1)


def P(cx, f, *a, **kw):
    if SELF_WAITS_A:
        cx.pool.wait(cx.nc.gpsimd)
    cx.pool.inc(f(*a, **kw), 1)


def dma_in(cx, ctr, dst, src, eng=None):
    ctr.inc((eng or cx.nc.sync).dma_start(dst, src), 16)


def dma_out(cx, ctr, dst, src, eng=None):
    ctr.inc((eng or cx.nc.gpsimd).dma_start(dst, src), 16)


def load_col(cx, ctr, dst_col, src_1d, lo, n=128):
    dma_in(cx, ctr, dst_col, src_1d[lo:lo + n].unsqueeze(1))


def bcast_row(cx, ctr, dst, src_1d, width, parts=128):
    dma_in(cx, ctr, dst[0:parts, 0:width],
           src_1d[0:width].unsqueeze(0).partition_broadcast(parts))



def emit_prep_transpose(cx, w_src, wt_dst, R, C, ident_bf16, sign=False,
                        colscale_1d=None):
    nc = cx.nc
    nR, nC = R // 128, C // 128
    GW = 4 if nC % 4 == 0 else 1
    with ExitStack() as st_:
        def lsb(name, shape, dtype):
            return st_.enter_context(nc.sbuf_tensor(uname(name), shape, dtype))
        wins = [lsb("prep_in", [128, C], F32) for _ in range(2)]
        wbs = [lsb("prep_b", [128, C], BF16) for _ in range(2)]
        wouts = [lsb("prep_out", [128, 512], BF16) for _ in range(2)]
        cs_cols = lsb("prep_cs", [128, nR], F32) if colscale_1d is not None else None
        pss = [st_.enter_context(nc.psum_tensor(uname("prep_ps"), [128, 512], BF16))
               for _ in range(2)]
        in_ctr = [cx.dma_ctr("prep_in0"), cx.dma_ctr("prep_in1")]
        out_ctr = [cx.dma_ctr("prep_out0", store=True),
                   cx.dma_ctr("prep_out1", store=True)]
        c_cs = cx.dma_ctr("prep_cs")
        if colscale_1d is not None:
            for rc in range(nR):
                load_col(cx, c_cs, cs_cols[:, rc:rc + 1], colscale_1d, rc * 128)
            c_cs.wait(nc.vector)
        pe_after = []
        for rc in range(nR):
            k = rc % 2
            if rc >= 2:
                cx.pe.wait(nc.sync, pe_after[rc - 2])
                cx.dve.wait(nc.sync)
                cx.act.wait(nc.sync)
            dma_in(cx, in_ctr[k], wins[k][:, :], w_src[rc * 128:(rc + 1) * 128, :])
            in_ctr[k].wait(nc.scalar)
            in_ctr[k].wait(nc.vector)
            if sign:
                cx.dve.wait(nc.scalar)
                if rc >= 2:
                    cx.pe.wait(nc.scalar, pe_after[rc - 2])
                A(cx, nc.scalar.activation, wbs[k][:, :], wins[k][:, :], AF.Sign,
                  bias=0.0, scale=1.0)
                if colscale_1d is not None:
                    cx.act.wait(nc.vector)
                    V(cx, nc.vector.tensor_scalar, wbs[k][:, :], wbs[k][:, :],
                      cs_cols[:, rc:rc + 1], None, ALU.mult)
            else:
                if rc >= 2:
                    cx.pe.wait(nc.vector, pe_after[rc - 2])
                if colscale_1d is not None:
                    V(cx, nc.vector.tensor_scalar, wbs[k][:, :], wins[k][:, :],
                      cs_cols[:, rc:rc + 1], None, ALU.mult)
                else:
                    V(cx, nc.vector.tensor_copy, wbs[k][:, :], wins[k][:, :])
            if sign and colscale_1d is None:
                cx.act.wait(nc.tensor)
            else:
                cx.dve.wait(nc.tensor)
            for cg in range(nC // GW):
                p = pss[cg % 2]
                wo = wouts[cg % 2]
                oc = out_ctr[cg % 2]
                cx.dve.wait(nc.tensor)
                for gi in range(GW):
                    cc = cg * GW + gi
                    m = nc.tensor.transpose(p[:, gi * 128:(gi + 1) * 128],
                                            wbs[k][:, cc * 128:(cc + 1) * 128],
                                            ident_bf16[:, :])
                    if gi == GW - 1:
                        cx.pe.inc(m, 1)
                cx.pe.wait(nc.vector)
                oc.wait(nc.vector)
                V(cx, nc.vector.tensor_copy, wo[:, 0:GW * 128], p[:, 0:GW * 128])
                cx.dve.wait(nc.gpsimd)
                dma_out(cx, oc,
                        wt_dst[rc, cg * GW:(cg + 1) * GW].transpose([1, 0, 2]),
                        wo[:, 0:GW * 128].rearrange("p (c r) -> p c r", c=GW))
            pe_after.append(cx.pe.val)
        for oc in out_ctr:
            oc.wait(nc.sync)
            oc.wait(nc.gpsimd)



def emit_ln_stats(cx, xt, n_free, stp, sq_scratch, affine_out_bf16=None,
                  out_f32=None, rms_only=False):
    nc = cx.nc
    s1, s2, mu, var, rstd = (stp[k] for k in ("s1", "s2", "mu", "var", "rstd"))
    inv = 1.0 / n_free
    cx.dve.wait(nc.scalar)
    A(cx, nc.scalar.activation, sq_scratch[:, :], xt[:, :], AF.Square,
      bias=0.0, scale=1.0)
    cx.act.wait(nc.vector)
    V(cx, nc.vector.reduce_sum, s2[:, 0:1], sq_scratch[:, :], axis=AX.X)
    if rms_only:
        V(cx, nc.vector.tensor_scalar, var[:, 0:1], s2[:, 0:1], inv, None,
          ALU.mult)
    else:
        V(cx, nc.vector.reduce_sum, s1[:, 0:1], xt[:, :], axis=AX.X)
        V(cx, nc.vector.tensor_scalar, mu[:, 0:1], s1[:, 0:1], inv, None,
          ALU.mult)
        cx.dve.wait(nc.scalar)
        A(cx, nc.scalar.activation, s1[:, 0:1], mu[:, 0:1], AF.Square,
          bias=0.0, scale=1.0)
        cx.act.wait(nc.vector)
        V(cx, nc.vector.tensor_scalar, var[:, 0:1], s2[:, 0:1], inv, None,
          ALU.mult)
        V(cx, nc.vector.tensor_tensor, var[:, 0:1], var[:, 0:1], s1[:, 0:1],
          ALU.subtract)
    cx.dve.wait(nc.scalar)
    A(cx, nc.scalar.activation, s2[:, 0:1], var[:, 0:1], AF.Sqrt,
      bias=stp["eps"][:, 0:1], scale=1.0)
    cx.act.wait(nc.vector)
    V(cx, nc.vector.reciprocal, rstd[:, 0:1], s2[:, 0:1])
    for out in (affine_out_bf16, out_f32):
        if out is None:
            continue
        if rms_only:
            V(cx, nc.vector.tensor_scalar, out[:, :], xt[:, :], rstd[:, 0:1],
              None, ALU.mult)
        else:
            V(cx, nc.vector.tensor_scalar, out[:, :], xt[:, :], mu[:, 0:1],
              rstd[:, 0:1], ALU.subtract, ALU.mult)


def mk_stat_pool(cx, prefix):
    d = {k: cx.sb(f"{prefix}_{k}", [128, 1], F32)
         for k in ("s1", "s2", "mu", "var", "rstd")}
    d["eps"] = cx.eps_col
    return d



def emit_ffn_segment(cx, x_in, x_out, wcfg, ident_f32, ident_bf16,
                     debug_stop=None):
    nc = cx.nc
    I = wcfg["I"]
    nI = I // 128
    swiglu = wcfg["kind"] == "swiglu"

    c_const = cx.dma_ctr("ffn_const")
    os_c = None
    if wcfg.get("out_s") is not None:
        os_c = cx.sb("ffn_os_c", [128, 1], F32)
        dma_in(cx, c_const, os_c[:, 0:1],
               wcfg["out_s"][0:1].unsqueeze(0).partition_broadcast(128))
        c_const.wait(nc.vector)

    sgn = not swiglu
    if swiglu:
        emit_prep_transpose(cx, wcfg["gate_w"], wcfg["gwT"], I, D, ident_bf16)
    emit_prep_transpose(cx, wcfg["up_w"], wcfg["uwT"], I, D, ident_bf16, sign=sgn)
    emit_prep_transpose(cx, wcfg["down_w"], wcfg["dwT"], D, I, ident_bf16,
                        sign=sgn, colscale_1d=wcfg["down_s"])

    if debug_stop == "prep":
        return

    gs_cols = cx.sb("ffn_gs", [128, nI], F32) if swiglu else None
    us_cols = cx.sb("ffn_us", [128, nI], F32)
    for ic in range(nI):
        if swiglu:
            load_col(cx, c_const, gs_cols[:, ic:ic + 1], wcfg["gate_s"], ic * 128)
        load_col(cx, c_const, us_cols[:, ic:ic + 1], wcfg["up_s"], ic * 128)
    c_const.wait(nc.scalar)
    c_const.wait(nc.vector)

    xt = cx.sb("ffn_xt", [128, D], F32)
    c_xt = cx.dma_ctr("ffn_xt")
    sq = cx.sb("ffn_sq", [128, D], F32)
    xn_tm = cx.sb("ffn_xntm", [128, D], BF16)
    xn_fm = cx.sb("ffn_xnfm", [128, 16 * 512], BF16)
    h_fm = cx.sb("ffn_hfm", [128, nI * 512], BF16)
    gu_w = [[cx.sb("ffn_gw", [128, 16 * 128], BF16),
             cx.sb("ffn_uw", [128, 16 * 128], BF16)] for _ in range(2)]
    c_guw = [cx.dma_ctr("ffn_guw0"), cx.dma_ctr("ffn_guw1")]
    nIh = nI // 2
    dw_sb = [cx.sb("ffn_dw", [128, nIh * 512], BF16) for _ in range(2)]
    c_dw = [cx.dma_ctr("ffn_dw0"), cx.dma_ctr("ffn_dw1")]
    sg = [cx.sb("ffn_sg", [128, 512], BF16) for _ in range(2)]
    uu = cx.sb("ffn_uu", [128, 512], BF16)
    res = [cx.sb(f"ffn_res{i}", [128, 512], F32) for i in range(2)]
    c_res = [cx.dma_ctr("ffn_res0"), cx.dma_ctr("ffn_res1")]
    out_t = [cx.sb(f"ffn_out{i}", [128, 512], F32) for i in range(2)]
    c_out = [cx.dma_ctr("ffn_out0", store=True), cx.dma_ctr("ffn_out1", store=True)]
    c_dbg = cx.dma_ctr("ffn_dbg", store=True) if debug_stop else None
    stp = mk_stat_pool(cx, "ffn")

    ps_t = [cx.ps(f"ffn_pst{i}", [128, 128], BF16) for i in range(2)]
    ps_g = [cx.ps(f"ffn_psg{i}", [128, 512]) for i in range(2)]
    ps_u = [cx.ps(f"ffn_psu{i}", [128, 512]) for i in range(2)]
    ps_d = [ps_g[0], ps_u[0], ps_g[1], ps_u[1]]

    def load_gu(ic, pe_after):
        k = ic % 2
        if ic >= 2:
            cx.pe.wait(nc.sync, pe_after[ic - 2])
        if swiglu:
            dma_in(cx, c_guw[k], gu_w[k][0][:, :].rearrange("p (c i) -> p c i", c=16),
                   wcfg["gwT"][ic].transpose([1, 0, 2]))
        dma_in(cx, c_guw[k], gu_w[k][1][:, :].rearrange("p (c i) -> p c i", c=16),
               wcfg["uwT"][ic].transpose([1, 0, 2]))

    def load_dw(half_idx, pe_after_dw, seq):
        k = half_idx % 2
        do, hf = seq[half_idx]
        if half_idx >= 2:
            cx.pe.wait(nc.sync, pe_after_dw[half_idx - 2])
        for o in range(4):
            oc = do * 4 + o
            dma_in(cx, c_dw[k],
                   dw_sb[k][:, :].rearrange("p (c o f) -> p o c f", c=nIh, o=4)[:, o],
                   wcfg["dwT"][oc, hf * nIh:(hf + 1) * nIh].transpose([1, 0, 2]))

    n_tc = TOK // 512
    for tc in range(n_tc):
        for ts in range(4):
            row0 = tc * 512 + ts * 128
            cx.dve.wait(nc.sync)
            cx.act.wait(nc.sync)
            dma_in(cx, c_xt, xt[:, :], x_in[row0:row0 + 128, :])
            c_xt.wait(nc.vector)
            c_xt.wait(nc.scalar)
            emit_ln_stats(cx, xt, D, stp, sq, affine_out_bf16=xn_tm)
            for dc in range(16):
                p = ps_t[dc % 2]
                cx.dve.wait(nc.tensor)
                cx.pe.inc(nc.tensor.transpose(
                    p[:, :], xn_tm[:, dc * 128:(dc + 1) * 128], ident_bf16[:, :]), 1)
                cx.pe.wait(nc.vector)
                V(cx, nc.vector.tensor_copy,
                  xn_fm[:, dc * 512 + ts * 128: dc * 512 + (ts + 1) * 128], p[:, :])

        if debug_stop == "ln":
            cx.dve.wait(nc.gpsimd)
            dma_out(cx, c_dbg, wcfg["dbg"][tc * 128:(tc + 1) * 128, :], xn_fm[:, :])
            c_dbg.wait(nc.vector)
            continue

        pe_after = []
        load_gu(0, pe_after)
        for ic in range(nI):
            k = ic % 2
            gw, uw = gu_w[k]
            if ic + 1 < nI:
                load_gu(ic + 1, pe_after)
            c_guw[k].wait(nc.tensor)
            pg, pu = ps_g[k], ps_u[k]
            cx.act.wait(nc.tensor)
            cx.dve.wait(nc.tensor)
            for dc in range(16):
                last = (dc == 15)
                if swiglu:
                    m = nc.tensor.matmul(pg[:, :], gw[:, dc * 128:(dc + 1) * 128],
                                         xn_fm[:, dc * 512:(dc + 1) * 512],
                                         start=(dc == 0), stop=last)
                    if last:
                        cx.pe.inc(m, 1)
                m = nc.tensor.matmul(pu[:, :], uw[:, dc * 128:(dc + 1) * 128],
                                     xn_fm[:, dc * 512:(dc + 1) * 512],
                                     start=(dc == 0), stop=last)
                if last:
                    cx.pe.inc(m, 1)
            pe_after.append(cx.pe.val)
            cx.pe.wait(nc.scalar)
            cx.pe.wait(nc.vector)
            if swiglu:
                cx.dve.wait(nc.scalar)
                A(cx, nc.scalar.activation, sg[k][:, :], pg[:, :], AF.Silu,
                  bias=0.0, scale=gs_cols[:, ic:ic + 1])
                V(cx, nc.vector.tensor_scalar, uu[:, :], pu[:, :],
                  us_cols[:, ic:ic + 1], None, ALU.mult)
                cx.act.wait(nc.vector)
                V(cx, nc.vector.tensor_tensor,
                  h_fm[:, ic * 512:(ic + 1) * 512], sg[k][:, :], uu[:, :], ALU.mult)
            else:
                A(cx, nc.scalar.activation,
                  h_fm[:, ic * 512:(ic + 1) * 512], pu[:, :], AF.Relu,
                  bias=0.0, scale=us_cols[:, ic:ic + 1])

        if debug_stop == "gu":
            cx.dve.wait(nc.gpsimd)
            cx.act.wait(nc.gpsimd)
            dma_out(cx, c_dbg, wcfg["dbg"][tc * 128:(tc + 1) * 128, :],
                    h_fm[:, 0:8192])
            c_dbg.wait(nc.vector)
            c_dbg.wait(nc.scalar)
            continue

        seq = [(do, hf) for do in range(4) for hf in range(2)]
        pe_after_dw = []
        load_dw(0, pe_after_dw, seq)
        for hi, (do, hf) in enumerate(seq):
            k = hi % 2
            if hi + 1 < len(seq):
                load_dw(hi + 1, pe_after_dw, seq)
            c_dw[k].wait(nc.tensor)
            for ts in range(4):
                row0 = tc * 512 + ts * 128
                kk = ts % 2
                p = ps_d[ts]
                if hf == 0:
                    cx.dve.wait(nc.tensor)
                    cx.act.wait(nc.tensor)
                else:
                    cx.dve.wait(nc.sync)
                    dma_in(cx, c_res[kk], res[kk][:, :],
                           x_in[row0:row0 + 128, do * 512:(do + 1) * 512])
                for ic in range(nIh):
                    m = nc.tensor.matmul(
                        p[:, :],
                        h_fm[:, (hf * nIh + ic) * 512 + ts * 128:
                             (hf * nIh + ic) * 512 + (ts + 1) * 128],
                        dw_sb[k][:, ic * 512:(ic + 1) * 512],
                        start=(hf == 0 and ic == 0), stop=(hf == 1 and ic == nIh - 1))
                    if ic == nIh - 1:
                        cx.pe.inc(m, 1)
                if hf == 1:
                    cx.pe.wait(nc.vector)
                    c_res[kk].wait(nc.vector)
                    c_out[kk].wait(nc.vector)
                    if os_c is not None:
                        V(cx, nc.vector.scalar_tensor_tensor, out_t[kk][:, :],
                          p[:, :], os_c[:, 0:1], res[kk][:, :], ALU.mult, ALU.add)
                    else:
                        V(cx, nc.vector.tensor_tensor,
                          out_t[kk][:, :], p[:, :], res[kk][:, :], ALU.add)
                    cx.dve.wait(nc.gpsimd)
                    dma_out(cx, c_out[kk],
                            x_out[row0:row0 + 128, do * 512:(do + 1) * 512],
                            out_t[kk][:, :])
            pe_after_dw.append(cx.pe.val)



def host_constants():
    return {
        "ident": np.eye(128, dtype=np.float32),
        "identb": np.eye(128, dtype=np.float32).astype(ml_dtypes.bfloat16),
    }


def declare_ffn_io(nc, layer, prefix=""):
    q = prefix
    if layer == 0:
        I = 5632
        d = {
            "I": I, "kind": "swiglu",
            "gate_w": nc.dram_tensor(q + "gate_w", [I, D], F32, kind="ExternalInput").ap(),
            "up_w": nc.dram_tensor(q + "up_w", [I, D], F32, kind="ExternalInput").ap(),
            "down_w": nc.dram_tensor(q + "down_w", [D, I], F32, kind="ExternalInput").ap(),
            "gate_s": nc.dram_tensor(q + "gate_s", [I], F32, kind="ExternalInput").ap(),
            "up_s": nc.dram_tensor(q + "up_s", [I], F32, kind="ExternalInput").ap(),
            "down_s": nc.dram_tensor(q + "down_s", [D], F32, kind="ExternalInput").ap(),
            "out_s": None,
            "gwT": nc.dram_tensor(q + "gwT", [I // 128, D // 128, 128, 128], BF16).ap(),
        }
    else:
        I = 8192
        d = {
            "I": I, "kind": "signrelu",
            "up_w": nc.dram_tensor(q + "up_w", [I, D], F32, kind="ExternalInput").ap(),
            "down_w": nc.dram_tensor(q + "down_w", [D, I], F32, kind="ExternalInput").ap(),
            "up_s": nc.dram_tensor(q + "up_s", [I], F32, kind="ExternalInput").ap(),
            "down_s": nc.dram_tensor(q + "down_s", [D], F32, kind="ExternalInput").ap(),
            "out_s": nc.dram_tensor(q + "out_s", [1], F32, kind="ExternalInput").ap(),
        }
    d["uwT"] = nc.dram_tensor(q + "uwT", [I // 128, D // 128, 128, 128], BF16).ap()
    d["dwT"] = nc.dram_tensor(q + "dwT", [D // 128, I // 128, 128, 128], BF16).ap()
    return d


def build_ffn_program(layer, debug_stop=None):
    nc = bass.Bass("TRN2", target_bir_lowering=False)
    x_in = nc.dram_tensor("x_in", [TOK, D], F32, kind="ExternalInput").ap()
    x_out = nc.dram_tensor("x_out", [TOK, D], F32, kind="ExternalOutput").ap()
    ident = nc.dram_tensor("ident", [128, 128], F32, kind="ExternalInput").ap()
    identb = nc.dram_tensor("identb", [128, 128], BF16, kind="ExternalInput").ap()
    wcfg = declare_ffn_io(nc, layer)
    if debug_stop:
        wcfg["dbg"] = nc.dram_tensor("dbg", [TOK // 4, 8192], BF16,
                                     kind="ExternalOutput").ap()
    with ExitStack() as stack:
        cx = Ctx(nc, stack)
        cx.init_consts()
        ident_sb = cx.sb("ident_sb", [128, 128], F32)
        identb_sb = cx.sb("identb_sb", [128, 128], BF16)
        c0 = cx.dma_ctr("ident")
        dma_in(cx, c0, ident_sb[:, :], ident[:, :])
        dma_in(cx, c0, identb_sb[:, :], identb[:, :])
        c0.wait(nc.tensor)
        emit_ffn_segment(cx, x_in, x_out, wcfg, ident_sb, identb_sb,
                         debug_stop=debug_stop)
        cx.final_barrier()
    return nc





def emit_prep_cast(cx, w_src, wb_dst, R, C):
    nc = cx.nc
    nR = R // 128
    with ExitStack() as st_:
        wins = [st_.enter_context(nc.sbuf_tensor(uname("prepc_in"), [128, C], F32))
                for _ in range(2)]
        wos = [st_.enter_context(nc.sbuf_tensor(uname("prepc_out"), [128, C], BF16))
               for _ in range(2)]
        in_ctr = [cx.dma_ctr("prepc_in0"), cx.dma_ctr("prepc_in1")]
        out_ctr = [cx.dma_ctr("prepc_out0", store=True),
                   cx.dma_ctr("prepc_out1", store=True)]
        for rc in range(nR):
            k = rc % 2
            if rc >= 2:
                cx.act.wait(nc.sync)
            dma_in(cx, in_ctr[k], wins[k][:, :], w_src[rc * 128:(rc + 1) * 128, :])
            in_ctr[k].wait(nc.scalar)
            out_ctr[k].wait(nc.scalar)
            A(cx, nc.scalar.activation, wos[k][:, :], wins[k][:, :], AF.Copy,
              bias=0.0, scale=1.0)
            cx.act.wait(nc.gpsimd)
            dma_out(cx, out_ctr[k], wb_dst[rc * 128:(rc + 1) * 128, :], wos[k][:, :])
        for oc in out_ctr:
            oc.wait(nc.sync)
            oc.wait(nc.gpsimd)



NUM_HEADS = 16
HEAD_DIM = 128
CHUNK = 128
NCHUNK = T // CHUNK


def attn_host_constants():
    import math
    H = NUM_HEADS
    start = 2.0 ** (-2.0 ** (-(math.log2(H) - 3)))
    slopes = np.array([start * start ** i for i in range(H)], dtype=np.float32)
    r = np.exp(-slopes).astype(np.float32)

    i = np.arange(CHUNK, dtype=np.float32)
    qdec = r[:, None] ** (i[None, :] + 1.0)
    kdec = r[:, None] ** (CHUNK - 1.0 - i[None, :])
    diff = i[:, None] - i[None, :]
    mask = diff >= 0
    dmask = np.where(mask[None], r[:, None, None] ** np.where(mask, diff, 0.0)[None], 0.0)
    r_C = (r ** float(CHUNK)).astype(np.float64)

    bf = ml_dtypes.bfloat16
    c = {}
    c["qdecT"] = np.broadcast_to(qdec.reshape(1, H * CHUNK), (128, H * CHUNK)).astype(bf)
    c["kdecT"] = np.broadcast_to(kdec.reshape(1, H * CHUNK), (128, H * CHUNK)).astype(bf)
    c["dmaskT"] = dmask.transpose(2, 0, 1).reshape(CHUNK, H * CHUNK).astype(bf)
    c["r_C"] = [float(v) for v in r_C]
    DH = HEAD_DIM
    freq = (1.0 / 1024.0) ** np.linspace(0.0, 1.0, DH // 4, dtype=np.float32)
    freq = np.concatenate([freq, np.zeros(DH // 4, dtype=np.float32)])
    t = np.arange(T, dtype=np.float32)
    theta = t[None, :] * freq[:, None]
    cosT = np.cos(theta)
    sinT = np.sin(theta)
    c["cosT"] = np.concatenate([cosT, cosT], 0).astype(bf)
    c["sinS"] = np.concatenate([sinT, -sinT], 0).astype(bf)
    psw = np.zeros((128, 128), dtype=np.float32)
    for m_ in range(128):
        psw[(m_ + 64) % 128, m_] = 1.0
    c["Pswap"] = psw.astype(bf)
    return c


def declare_attn_io(nc, prefix=""):
    q = prefix
    d = {
        "Wq": nc.dram_tensor(q + "Wq", [D, D], F32, kind="ExternalInput").ap(),
        "Wk": nc.dram_tensor(q + "Wk", [D, D], F32, kind="ExternalInput").ap(),
        "Wv": nc.dram_tensor(q + "Wv", [D, D], F32, kind="ExternalInput").ap(),
        "Wo": nc.dram_tensor(q + "Wo", [D, D], F32, kind="ExternalInput").ap(),
        "rms_w": nc.dram_tensor(q + "rms_w", [D], F32, kind="ExternalInput").ap(),
        "qdecT": nc.dram_tensor(q + "qdecT", [128, NUM_HEADS * 128], BF16, kind="ExternalInput").ap(),
        "kdecT": nc.dram_tensor(q + "kdecT", [128, NUM_HEADS * 128], BF16, kind="ExternalInput").ap(),
        "dmaskT": nc.dram_tensor(q + "dmaskT", [128, NUM_HEADS * 128], BF16, kind="ExternalInput").ap(),
        "cosT": nc.dram_tensor(q + "cosT", [128, T], BF16, kind="ExternalInput").ap(),
        "sinS": nc.dram_tensor(q + "sinS", [128, T], BF16, kind="ExternalInput").ap(),
        "Pswap": nc.dram_tensor(q + "Pswap", [128, 128], BF16, kind="ExternalInput").ap(),
        "WqB": nc.dram_tensor(q + "WqB", [D, D], BF16).ap(),
        "WkB": nc.dram_tensor(q + "WkB", [D, D], BF16).ap(),
        "WvB": nc.dram_tensor(q + "WvB", [D, D], BF16).ap(),
        "WoB": nc.dram_tensor(q + "WoB", [D, D], BF16).ap(),
    }
    return d


def emit_attn_segment(cx, x_in, x_out, w, r_C, ident_bf16):
    nc = cx.nc
    H = NUM_HEADS

    emit_prep_cast(cx, w["Wq"], w["WqB"], D, D)
    emit_prep_cast(cx, w["Wk"], w["WkB"], D, D)
    emit_prep_cast(cx, w["Wv"], w["WvB"], D, D)
    emit_prep_cast(cx, w["Wo"], w["WoB"], D, D)

    c_const = cx.dma_ctr("attn_const")
    qdec_h = cx.sb("qdec_h", [128, 128], BF16)
    kdec_h = cx.sb("kdec_h", [128, 128], BF16)
    dmask_h = cx.sb("dmask_h", [128, 128], BF16)
    c_dec = cx.dma_ctr("attn_dec")
    cosT = cx.sb("cosT", [128, T], BF16)
    sinS = cx.sb("sinS", [128, T], BF16)
    pswap = cx.sb("pswap", [128, 128], BF16)
    rmsw_b = cx.sb("rmsw_b", [128, D], BF16)
    dma_in(cx, c_const, cosT[:, :], w["cosT"][:, :])
    dma_in(cx, c_const, sinS[:, :], w["sinS"][:, :])
    dma_in(cx, c_const, pswap[:, :], w["Pswap"][:, :])
    c_const.wait(nc.tensor)
    c_rmsw = cx.dma_ctr("attn_rmsw")
    dma_in(cx, c_rmsw, rmsw_b[0:128, 0:D],
           w["rms_w"][0:D].unsqueeze(0).partition_broadcast(128), eng=nc.gpsimd)
    c_rmsw.wait(nc.vector)
    c_const.wait(nc.vector)
    c_const.wait(nc.scalar)

    nTC = T // 128
    nSL = T // 512
    big_fm = cx.sb("a_bigfm", [128, 16 * T], BF16)
    attn_tm = cx.sb("a_attntm", [128, nTC * D], BF16)
    xt = cx.sb("a_xt", [128, D], F32)
    c_xt = cx.dma_ctr("a_xt")
    sq = cx.sb("a_sq", [128, D], F32)
    xn_tm = cx.sb("a_xntm", [128, D], BF16)
    stp = mk_stat_pool(cx, "a_ln")

    qf = cx.sb("a_qf", [128, T], BF16)
    kf = cx.sb("a_kf", [128, T], BF16)
    vt = cx.sb("a_vt", [128, T], BF16)
    rot_t = cx.sb("a_rott", [128, 512], BF16)
    qraw = cx.sb("a_qraw", [128, 512], BF16)
    wslab = cx.sb("a_wslab", [128, 16 * 512], BF16)
    whead = [wslab[:, i * 2048:(i + 1) * 2048] for i in range(3)]
    c_wh = cx.dma_ctr("a_wh")
    kv_f = [cx.sb(f"a_kvf{i}", [128, 128], F32) for i in range(2)]
    kv_b = cx.sb("a_kvb", [128, 128], BF16)
    qd_t = cx.sb("a_qd", [128, 128], BF16)
    kd_t = cx.sb("a_kd", [128, 128], BF16)
    kdtm = cx.sb("a_kdtm", [128, 128], BF16)
    sT_t = cx.sb("a_sT", [128, 128], BF16)
    wo_sl = wslab
    c_wo = cx.dma_ctr("a_wo")
    res = [cx.sb(f"a_res{i}", [128, 512], F32) for i in range(2)]
    c_res = [cx.dma_ctr("a_res0"), cx.dma_ctr("a_res1")]
    out_t = [cx.sb(f"a_out{i}", [128, 512], F32) for i in range(2)]
    c_out = [cx.dma_ctr("a_out0", store=True), cx.dma_ctr("a_out1", store=True)]

    ps_p = cx.ps("a_psp", [128, 512])
    ps_v = cx.ps("a_psv", [128, 128])
    ps_sc = cx.ps("a_pssc", [128, 128])
    ps_at = cx.ps("a_psat", [128, 128])
    ps_tr = cx.ps("a_pstr", [128, 128], BF16)
    ps_kv = cx.ps("a_pskv", [128, 128])

    for b in range(B_PER_CORE):
        base = b * T
        for ts in range(nTC):
            row0 = base + ts * 128
            cx.dve.wait(nc.sync)
            cx.act.wait(nc.sync)
            dma_in(cx, c_xt, xt[:, :], x_in[row0:row0 + 128, :])
            c_xt.wait(nc.vector)
            c_xt.wait(nc.scalar)
            emit_ln_stats(cx, xt, D, stp, sq, affine_out_bf16=xn_tm)
            for dc in range(16):
                cx.dve.wait(nc.tensor)
                cx.pe.inc(nc.tensor.transpose(
                    ps_tr[:, :], xn_tm[:, dc * 128:(dc + 1) * 128], ident_bf16[:, :]), 1)
                cx.pe.wait(nc.vector)
                V(cx, nc.vector.tensor_copy,
                  big_fm[:, dc * T + ts * 128: dc * T + (ts + 1) * 128], ps_tr[:, :])

        for h in range(H):
            cx.pe.wait(nc.sync)
            cx.dve.wait(nc.sync)
            dma_in(cx, c_dec, qdec_h[:, :], w["qdecT"][:, h * 128:(h + 1) * 128])
            dma_in(cx, c_dec, kdec_h[:, :], w["kdecT"][:, h * 128:(h + 1) * 128])
            dma_in(cx, c_dec, dmask_h[:, :], w["dmaskT"][:, h * 128:(h + 1) * 128])
            for wi, wname in enumerate(("WqB", "WkB", "WvB")):
                dma_in(cx, c_wh,
                       whead[wi][:, :].rearrange("p (c e) -> p c e", c=16),
                       w[wname][:, h * 128:(h + 1) * 128]
                       .rearrange("(c p) e -> c p e", p=128).transpose([1, 0, 2]))
            c_wh.wait(nc.tensor)

            for which, dst in ((0, qf), (1, kf)):
                for sl in range(nSL):
                    t0 = sl * 512
                    cx.dve.wait(nc.tensor)
                    for dc in range(16):
                        m = nc.tensor.matmul(
                            ps_p[:, :], whead[which][:, dc * 128:(dc + 1) * 128],
                            big_fm[:, dc * T + t0: dc * T + t0 + 512],
                            start=(dc == 0), stop=(dc == 15))
                        if dc == 15:
                            cx.pe.inc(m, 1)
                    cx.pe.wait(nc.vector)
                    V(cx, nc.vector.tensor_copy, qraw[:, :], ps_p[:, :])
                    cx.dve.wait(nc.tensor)
                    cx.pe.inc(nc.tensor.matmul(ps_p[:, :], pswap[:, :],
                                               qraw[:, :], start=True, stop=True), 1)
                    cx.pe.wait(nc.vector)
                    V(cx, nc.vector.tensor_tensor, rot_t[:, :],
                      ps_p[:, :], sinS[:, t0:t0 + 512], ALU.mult)
                    V(cx, nc.vector.tensor_tensor, dst[:, t0:t0 + 512],
                      qraw[:, :], cosT[:, t0:t0 + 512], ALU.mult)
                    V(cx, nc.vector.tensor_tensor, dst[:, t0:t0 + 512],
                      dst[:, t0:t0 + 512], rot_t[:, :], ALU.add)

            for tcc in range(nTC):
                cx.dve.wait(nc.tensor)
                for dc in range(16):
                    m = nc.tensor.matmul(
                        ps_v[:, :], big_fm[:, dc * T + tcc * 128: dc * T + (tcc + 1) * 128],
                        whead[2][:, dc * 128:(dc + 1) * 128],
                        start=(dc == 0), stop=(dc == 15))
                    if dc == 15:
                        cx.pe.inc(m, 1)
                cx.pe.wait(nc.vector)
                V(cx, nc.vector.tensor_copy, vt[:, tcc * 128:(tcc + 1) * 128], ps_v[:, :])

            V(cx, nc.vector.memset, kv_f[1][:, :], 0.0)
            V(cx, nc.vector.memset, kv_b[:, :], 0.0)

            for ci in range(NCHUNK):
                c0 = ci * 128
                kcur = kv_f[ci % 2]
                kprev = kv_f[(ci + 1) % 2]
                cx.dve.wait(nc.tensor)
                cx.pe.inc(nc.tensor.matmul(ps_sc[:, :], kf[:, c0:c0 + 128],
                                           qf[:, c0:c0 + 128], start=True, stop=True), 1)
                cx.pe.wait(nc.vector)
                if ci == 0:
                    c_dec.wait(nc.vector)
                V(cx, nc.vector.tensor_tensor, sT_t[:, :], ps_sc[:, :],
                  dmask_h[:, :], ALU.mult)
                V(cx, nc.vector.tensor_tensor, qd_t[:, :], qf[:, c0:c0 + 128],
                  qdec_h[:, :], ALU.mult)
                cx.dve.wait(nc.tensor)
                if ci > 0:
                    cx.pe.inc(nc.tensor.matmul(ps_at[:, :], qd_t[:, :], kv_b[:, :],
                                               start=True, stop=False), 1)
                cx.pe.inc(nc.tensor.matmul(ps_at[:, :], sT_t[:, :],
                                           vt[:, c0:c0 + 128],
                                           start=(ci == 0), stop=True), 1)
                cx.pe.wait(nc.vector)
                V(cx, nc.vector.tensor_copy,
                  attn_tm[:, ci * D + h * 128: ci * D + (h + 1) * 128], ps_at[:, :])
                V(cx, nc.vector.tensor_tensor, kd_t[:, :], kf[:, c0:c0 + 128],
                  kdec_h[:, :], ALU.mult)
                cx.dve.wait(nc.tensor)
                cx.pe.inc(nc.tensor.transpose(ps_tr[:, :], kd_t[:, :],
                                              ident_bf16[:, :]), 1)
                cx.pe.wait(nc.vector)
                V(cx, nc.vector.tensor_copy, kdtm[:, :], ps_tr[:, :])
                cx.dve.wait(nc.tensor)
                cx.pe.inc(nc.tensor.matmul(ps_kv[:, :], kdtm[:, :],
                                           vt[:, c0:c0 + 128], start=True, stop=True), 1)
                cx.pe.wait(nc.vector)
                V(cx, nc.vector.scalar_tensor_tensor, kcur[:, :], kprev[:, :],
                  r_C[h], ps_kv[:, :], ALU.mult, ALU.add)
                V(cx, nc.vector.tensor_copy, kv_b[:, :], kcur[:, :])

        for tcc in range(nTC):
            att_sl = attn_tm[:, tcc * D:(tcc + 1) * D]
            emit_ln_stats(cx, att_sl, D, stp, sq, rms_only=True)
            V(cx, nc.vector.scalar_tensor_tensor, xn_tm[:, :], att_sl,
              stp["rstd"][:, 0:1], rmsw_b[:, :], ALU.mult, ALU.mult)
            for ec in range(16):
                cx.dve.wait(nc.tensor)
                cx.pe.inc(nc.tensor.transpose(
                    ps_tr[:, :], xn_tm[:, ec * 128:(ec + 1) * 128], ident_bf16[:, :]), 1)
                cx.pe.wait(nc.vector)
                V(cx, nc.vector.tensor_copy,
                  big_fm[:, ec * T + tcc * 128: ec * T + (tcc + 1) * 128], ps_tr[:, :])

        for do in range(4):
            cx.pe.wait(nc.sync)
            dma_in(cx, c_wo, wo_sl[:, :].rearrange("p (c f) -> p c f", c=16),
                   w["WoB"][:, do * 512:(do + 1) * 512]
                   .rearrange("(c p) f -> c p f", p=128).transpose([1, 0, 2]))
            c_wo.wait(nc.tensor)
            for tcc in range(nTC):
                row0 = base + tcc * 128
                k = tcc % 2
                cx.dve.wait(nc.sync)
                dma_in(cx, c_res[k], res[k][:, :],
                       x_in[row0:row0 + 128, do * 512:(do + 1) * 512])
                cx.dve.wait(nc.tensor)
                for ec in range(16):
                    m = nc.tensor.matmul(
                        ps_p[:, :], big_fm[:, ec * T + tcc * 128: ec * T + (tcc + 1) * 128],
                        wo_sl[:, ec * 512:(ec + 1) * 512],
                        start=(ec == 0), stop=(ec == 15))
                    if ec == 15:
                        cx.pe.inc(m, 1)
                cx.pe.wait(nc.vector)
                c_res[k].wait(nc.vector)
                c_out[k].wait(nc.vector)
                V(cx, nc.vector.tensor_tensor, out_t[k][:, :], ps_p[:, :],
                  res[k][:, :], ALU.add)
                cx.dve.wait(nc.gpsimd)
                dma_out(cx, c_out[k],
                        x_out[row0:row0 + 128, do * 512:(do + 1) * 512], out_t[k][:, :])


def build_attn_program():
    nc = bass.Bass("TRN2", target_bir_lowering=False)
    x_in = nc.dram_tensor("x_in", [TOK, D], F32, kind="ExternalInput").ap()
    x_out = nc.dram_tensor("x_out", [TOK, D], F32, kind="ExternalOutput").ap()
    ident = nc.dram_tensor("ident", [128, 128], F32, kind="ExternalInput").ap()
    identb = nc.dram_tensor("identb", [128, 128], BF16, kind="ExternalInput").ap()
    w = declare_attn_io(nc)
    hc = attn_host_constants()
    with ExitStack() as stack:
        cx = Ctx(nc, stack)
        cx.init_consts()
        ident_sb = cx.sb("ident_sb", [128, 128], F32)
        identb_sb = cx.sb("identb_sb", [128, 128], BF16)
        c0 = cx.dma_ctr("ident")
        dma_in(cx, c0, ident_sb[:, :], ident[:, :])
        dma_in(cx, c0, identb_sb[:, :], identb[:, :])
        c0.wait(nc.tensor)
        emit_attn_segment(cx, x_in, x_out, w, hc["r_C"], identb_sb)
        cx.final_barrier()
    return nc





STATE = 64
DT_RANK = 128
SSM_CHUNK = 512


def scan_host_constants():
    bf = ml_dtypes.bfloat16
    c = {}
    Eg = np.zeros((8, 128, 128), dtype=np.float32)
    for gp in range(8):
        for p in range(128):
            Eg[gp, 16 * gp + p // 8, p] = 1.0
    c["Eg"] = Eg.transpose(1, 0, 2).reshape(128, 8 * 128).astype(bf)
    EB = np.zeros((8, 128, 128), dtype=np.float32)
    EC = np.zeros((8, 128, 128), dtype=np.float32)
    for j in range(8):
        for p in range(128):
            EB[j, 8 * j + p % 8, p] = 1.0
            EC[j, 64 + 8 * j + p % 8, p] = 1.0
    c["EB"] = EB.transpose(1, 0, 2).reshape(128, 8 * 128).astype(bf)
    c["EC"] = EC.transpose(1, 0, 2).reshape(128, 8 * 128).astype(bf)
    Er = np.zeros((8, 128, 128), dtype=np.float32)
    for gs in range(8):
        for m in range(16 * gs, 16 * gs + 16):
            for p in range(128):
                if p // 8 == m % 16:
                    Er[gs, p, m] = 1.0
    c["Ered"] = Er.transpose(1, 0, 2).reshape(128, 8 * 128).astype(bf)
    return c


def rearrange_A_log(A_log):
    return np.ascontiguousarray(
        A_log.reshape(128, 16, 8, 8).transpose(1, 3, 0, 2).reshape(128, 1024))


def declare_scan_io(nc, prefix=""):
    q = prefix
    d = {
        "norm_w": nc.dram_tensor(q + "norm_w", [D], F32, kind="ExternalInput").ap(),
        "norm_b": nc.dram_tensor(q + "norm_b", [D], F32, kind="ExternalInput").ap(),
        "x_proj_w": nc.dram_tensor(q + "x_proj_w", [256, D], F32, kind="ExternalInput").ap(),
        "dt_proj_w": nc.dram_tensor(q + "dt_proj_w", [D, DT_RANK], F32, kind="ExternalInput").ap(),
        "dt_proj_b": nc.dram_tensor(q + "dt_proj_b", [D], F32, kind="ExternalInput").ap(),
        "A_log_sc": nc.dram_tensor(q + "A_log_sc", [128, 1024], F32, kind="ExternalInput").ap(),
        "D_param": nc.dram_tensor(q + "D_param", [D], F32, kind="ExternalInput").ap(),
        "out_proj_w": nc.dram_tensor(q + "out_proj_w", [D, D], F32, kind="ExternalInput").ap(),
        "Eg": nc.dram_tensor(q + "Eg", [128, 8 * 128], BF16, kind="ExternalInput").ap(),
        "EB": nc.dram_tensor(q + "EB", [128, 8 * 128], BF16, kind="ExternalInput").ap(),
        "EC": nc.dram_tensor(q + "EC", [128, 8 * 128], BF16, kind="ExternalInput").ap(),
        "Ered": nc.dram_tensor(q + "Ered", [128, 8 * 128], BF16, kind="ExternalInput").ap(),
        "xpwT": nc.dram_tensor(q + "xpwT", [2, 16, 128, 128], BF16).ap(),
        "dtpwT": nc.dram_tensor(q + "dtpwT", [16, 1, 128, 128], BF16).ap(),
        "opwT": nc.dram_tensor(q + "opwT", [16, 16, 128, 128], BF16).ap(),
        "xn2b_d": nc.dram_tensor(q + "xn2b_d", [16, 128, T], BF16).ap(),
        "x1d_d": nc.dram_tensor(q + "x1d_d", [16, 128, T], F32).ap(),
        "dthi_d": nc.dram_tensor(q + "dthi_d", [16, 128, T], BF16).ap(),
        "dtlo_d": nc.dram_tensor(q + "dtlo_d", [16, 128, T], BF16).ap(),
        "dtx_d": nc.dram_tensor(q + "dtx_d", [16, 128, T], BF16).ap(),
        "bcrep_d": nc.dram_tensor(q + "bcrep_d", [8, 128, 2 * T], BF16).ap(),
        "pre_d": nc.dram_tensor(q + "pre_d", [16, 128, T], BF16).ap(),
    }
    return d


def emit_scan_segment(cx, x_in, x_out, w, ident_f32, ident_bf16):
    nc = cx.nc
    nTC = T // 128
    nSL = T // 512

    emit_prep_transpose(cx, w["x_proj_w"], w["xpwT"], 256, D, ident_bf16)
    emit_prep_transpose(cx, w["dt_proj_w"], w["dtpwT"], D, DT_RANK, ident_bf16)
    emit_prep_transpose(cx, w["out_proj_w"], w["opwT"], D, D, ident_bf16)

    c_const = cx.dma_ctr("sc_const")
    c_cast = cx.dma_ctr("sc_cast")
    Eg = cx.sb("sc_Eg", [128, 8 * 128], BF16)
    Ered = cx.sb("sc_Ered", [128, 8 * 128], BF16)
    ebc_t = cx.sb("sc_ebc", [128, 256], BF16)
    c_ebc = cx.dma_ctr("sc_ebc")
    dma_in(cx, c_const, Eg[:, :], w["Eg"][:, :])
    dma_in(cx, c_const, Ered[:, :], w["Ered"][:, :])
    nw_b = cx.sb("sc_nw", [128, D], BF16)
    nb_b = cx.sb("sc_nb", [128, D], BF16)
    dp_b = cx.sb("sc_dp", [128, D], BF16)
    dma_in(cx, c_cast, nw_b[0:128, 0:D],
           w["norm_w"][0:D].unsqueeze(0).partition_broadcast(128), eng=nc.gpsimd)
    dma_in(cx, c_cast, nb_b[0:128, 0:D],
           w["norm_b"][0:D].unsqueeze(0).partition_broadcast(128), eng=nc.gpsimd)
    dma_in(cx, c_cast, dp_b[0:128, 0:D],
           w["D_param"][0:D].unsqueeze(0).partition_broadcast(128), eng=nc.gpsimd)
    dtb_cols = cx.sb("sc_dtb", [128, 16], F32)
    for dc in range(16):
        load_col(cx, c_const, dtb_cols[:, dc:dc + 1], w["dt_proj_b"], dc * 128)
    a_neg = cx.sb("sc_aneg", [128, 1024], F32)
    with nc.sbuf_tensor(uname("sc_alog"), [128, 1024], F32) as alog:
        c_alog = cx.dma_ctr("sc_alog")
        dma_in(cx, c_alog, alog[:, :], w["A_log_sc"][:, :])
        c_const.wait(nc.scalar)
        c_const.wait(nc.vector)
        c_const.wait(nc.tensor)
        c_cast.wait(nc.vector)
        c_alog.wait(nc.scalar)
        A(cx, nc.scalar.activation, a_neg[:, :], alog[:, :], AF.Exp, bias=0.0, scale=1.0)
        cx.act.wait(nc.vector)
        V(cx, nc.vector.tensor_scalar, a_neg[:, :], a_neg[:, :], -1.0, None, ALU.mult)
        cx.dve.wait(nc.scalar)

    xpw_sb = cx.sb("sc_xpw", [128, 16 * 256], BF16)
    dtpw_sb = cx.sb("sc_dtpw", [128, 16 * 128], BF16)
    c_wsl = cx.dma_ctr("sc_wsl")
    cx.act.wait(nc.sync)
    cx.dve.wait(nc.sync)
    for r_ in range(2):
        dma_in(cx, c_wsl,
               xpw_sb[:, :].rearrange("p (c r f) -> p c r f", c=16, r=2)[:, :, r_, :],
               w["xpwT"][r_].transpose([1, 0, 2]))
    dma_in(cx, c_wsl, dtpw_sb[:, :].rearrange("p (c f) -> p c f", c=16),
           w["dtpwT"][:, 0].transpose([1, 0, 2]))
    c_wsl.wait(nc.tensor)

    stp = mk_stat_pool(cx, "sc")
    xt = cx.sb("sc_xt", [128, D], F32)
    c_xt = cx.dma_ctr("sc_xt")
    sq = cx.sb("sc_sq", [128, D], F32)
    xn1 = cx.sb("sc_xn1", [128, D], F32)
    xn2b = cx.sb("sc_xn2b", [128, D], BF16)
    x1d_t = cx.sb("sc_x1d", [128, D], F32)
    delta_fm = cx.sb("sc_delta", [128, T], BF16)
    bc_fm = cx.sb("sc_bc", [128, T], BF16)
    projtm = cx.sb("sc_projtm", [128, 256], BF16)
    xn2sl = [cx.sb("sc_xn2sl0", [128, 16 * 128], BF16)]
    c_xn2sl = [cx.dma_ctr("sc_xn2sl0")]
    wt1 = cx.sb("sc_wt1", [128, 512], F32)
    wt2 = cx.sb("sc_wt2", [128, 512], F32)
    dt_f = cx.sb("sc_dtf", [128, 512], F32)
    dthl = cx.sb("sc_dthl", [128, 512], BF16)
    xn2ld = cx.sb("sc_xn2ld", [128, 512], BF16)
    c_xn2ld = cx.dma_ctr("sc_xn2ld")
    dt3 = [cx.sb("sc_dt3_0", [128, 3 * T], BF16)]
    c_dt3 = [cx.dma_ctr("sc_dt30")]
    dtrep = [cx.sb("sc_dtrep", [128, T], F32) for _ in range(2)]
    dtxrep = [cx.sb("sc_dtxrep", [128, T], BF16) for _ in range(2)]
    dA_sb = [cx.sb("sc_dA", [128, T], F32) for _ in range(2)]
    u_sb = cx.sb("sc_u", [128, T], BF16)
    h_sb = [cx.sb("sc_h", [128, T], BF16) for _ in range(2)]
    g_sb = [cx.sb("sc_g", [128, T], BF16) for _ in range(2)]
    bct = [cx.sb(f"sc_bct{i}", [128, 2 * T], BF16) for i in range(2)]
    brt = [t[:, 0:T] for t in bct]
    crt = [t[:, T:2 * T] for t in bct]
    c_br = [cx.dma_ctr("sc_br0"), cx.dma_ctr("sc_br1")]
    x1dt = [cx.sb(f"sc_x1dt{i}", [128, 512], F32) for i in range(2)]
    c_x1dt = [cx.dma_ctr("sc_x1dt0"), cx.dma_ctr("sc_x1dt1")]
    pre_t = cx.sb("sc_pret", [128, 512], BF16)
    wslab = cx.sb("sc_wslab", [128, 16 * 512], BF16)
    c_wo = cx.dma_ctr("sc_wo")
    preslab = [cx.sb("sc_presl0", [128, 16 * 128], BF16)]
    c_pre = [cx.dma_ctr("sc_pre0")]
    res = [cx.sb("sc_res0", [128, 512], F32)]
    c_res = [cx.dma_ctr("sc_res0")]
    out_t = [cx.sb("sc_out0", [128, 512], F32)]
    c_out = [cx.dma_ctr("sc_out0", store=True)]
    c_st = cx.dma_ctr("sc_store", store=True)

    ps_tb = cx.ps("sc_pstb", [128, 512], BF16)
    ps_pj = cx.ps("sc_pspj", [128, 512])
    ps_y = [cx.ps(f"sc_psy{i}", [128, 512]) for i in range(4)]

    dve_after_scan = {}
    act_after_rep = {}
    pe_after_y = {}
    dve_after_g = {}
    dve_after_cr = {}
    dve_after_c5 = {}

    for b in range(B_PER_CORE):
        base = b * T
        for ts in range(nTC):
            row0 = base + ts * 128
            cx.dve.wait(nc.sync)
            cx.act.wait(nc.sync)
            dma_in(cx, c_xt, xt[:, :], x_in[row0:row0 + 128, :])
            c_xt.wait(nc.vector)
            c_xt.wait(nc.scalar)
            emit_ln_stats(cx, xt, D, stp, sq, out_f32=xn1)
            V(cx, nc.vector.tensor_tensor, x1d_t[:, :], xn1[:, :], dp_b[:, :], ALU.mult)
            for cg in range(4):
                cx.dve.wait(nc.tensor)
                cx.act.wait(nc.tensor)
                for gi in range(4):
                    dc = cg * 4 + gi
                    m = nc.tensor.transpose(ps_pj[:, gi * 128:(gi + 1) * 128],
                                            x1d_t[:, dc * 128:(dc + 1) * 128],
                                            ident_f32[:, :])
                    if gi == 3:
                        cx.pe.inc(m, 1)
                cx.pe.wait(nc.vector)
                c_st.wait(nc.vector)
                V(cx, nc.vector.tensor_copy, wt1[:, :], ps_pj[:, :])
                cx.dve.wait(nc.gpsimd)
                dma_out(cx, c_st,
                        w["x1d_d"][cg * 4:(cg + 1) * 4, :, ts * 128:(ts + 1) * 128]
                        .transpose([1, 0, 2]),
                        wt1[:, :].rearrange("p (c r) -> p c r", c=4))
            emit_ln_stats(cx, xn1, D, stp, sq, out_f32=xt)
            V(cx, nc.vector.tensor_tensor, xt[:, :], xt[:, :], nw_b[:, :], ALU.mult)
            V(cx, nc.vector.tensor_tensor, xn2b[:, :], xt[:, :], nb_b[:, :], ALU.add)
            for cg in range(4):
                cx.dve.wait(nc.tensor)
                for gi in range(4):
                    dc = cg * 4 + gi
                    m = nc.tensor.transpose(ps_tb[:, gi * 128:(gi + 1) * 128],
                                            xn2b[:, dc * 128:(dc + 1) * 128],
                                            ident_bf16[:, :])
                    if gi == 3:
                        cx.pe.inc(m, 1)
                cx.pe.wait(nc.vector)
                c_st.wait(nc.vector)
                V(cx, nc.vector.tensor_copy, dthl[:, :], ps_tb[:, :])
                cx.dve.wait(nc.gpsimd)
                dma_out(cx, c_st,
                        w["xn2b_d"][cg * 4:(cg + 1) * 4, :, ts * 128:(ts + 1) * 128]
                        .transpose([1, 0, 2]),
                        dthl[:, :].rearrange("p (c r) -> p c r", c=4))
        c_st.wait(nc.sync)

        for ts in range(nTC):
            k = 0
            cx.pe.wait(nc.sync)
            dma_in(cx, c_xn2sl[k], xn2sl[k][:, :].rearrange("p (c t) -> p c t", c=16),
                   w["xn2b_d"][:, :, ts * 128:(ts + 1) * 128].transpose([1, 0, 2]))
            c_xn2sl[k].wait(nc.tensor)
            cx.dve.wait(nc.tensor)
            for dc in range(16):
                m = nc.tensor.matmul(ps_pj[:, 0:256],
                                     xn2sl[k][:, dc * 128:(dc + 1) * 128],
                                     xpw_sb[:, dc * 256:(dc + 1) * 256],
                                     start=(dc == 0), stop=(dc == 15))
                if dc == 15:
                    cx.pe.inc(m, 1)
            cx.pe.wait(nc.vector)
            V(cx, nc.vector.tensor_copy, projtm[:, :], ps_pj[:, 0:256])
            cx.dve.wait(nc.tensor)
            cx.pe.inc(nc.tensor.transpose(ps_tb[:, 0:128], projtm[:, 0:128],
                                          ident_bf16[:, :]), 1)
            cx.pe.inc(nc.tensor.transpose(ps_tb[:, 128:256], projtm[:, 128:256],
                                          ident_bf16[:, :]), 1)
            cx.pe.wait(nc.vector)
            V(cx, nc.vector.tensor_copy, delta_fm[:, ts * 128:(ts + 1) * 128],
              ps_tb[:, 0:128])
            V(cx, nc.vector.tensor_copy, bc_fm[:, ts * 128:(ts + 1) * 128],
              ps_tb[:, 128:256])

        for j in range(8):
            cx.pe.wait(nc.sync)
            dma_in(cx, c_ebc, ebc_t[:, 0:128], w["EB"][:, j * 128:(j + 1) * 128])
            dma_in(cx, c_ebc, ebc_t[:, 128:256], w["EC"][:, j * 128:(j + 1) * 128])
            c_ebc.wait(nc.tensor)
            for which in (0, 1):
                sel = ebc_t[:, which * 128:(which + 1) * 128]
                for sl in range(nSL):
                    rhs = bc_fm[:, sl * 512:(sl + 1) * 512] if which == 0                         else bc_fm[:, 0:512]
                    cx.dve.wait(nc.tensor)
                    cx.pe.inc(nc.tensor.matmul(
                        ps_pj[:, :], sel, rhs,
                        start=True, stop=True), 1)
                    cx.pe.wait(nc.vector)
                    c_st.wait(nc.vector)
                    V(cx, nc.vector.tensor_copy, dthl[:, :], ps_pj[:, :])
                    cx.dve.wait(nc.gpsimd)
                    dma_out(cx, c_st,
                            w["bcrep_d"][j, :, which * T + sl * 512:
                                         which * T + (sl + 1) * 512],
                            dthl[:, :])
        c_st.wait(nc.sync)

        for dc in range(16):
            for sl in range(nSL):
                cx.dve.wait(nc.sync)
                dma_in(cx, c_xn2ld, xn2ld[:, :],
                       w["xn2b_d"][dc, :, sl * 512:(sl + 1) * 512])
                cx.act.wait(nc.tensor)
                cx.dve.wait(nc.tensor)
                cx.pe.inc(nc.tensor.matmul(
                    ps_pj[:, :], dtpw_sb[:, dc * 128:(dc + 1) * 128],
                    delta_fm[:, sl * 512:(sl + 1) * 512],
                    start=True, stop=True), 1)
                cx.pe.wait(nc.scalar)
                A(cx, nc.scalar.activation, wt1[:, :], ps_pj[:, :], AF.Identity,
                  bias=dtb_cols[:, dc:dc + 1], scale=1.0)
                A(cx, nc.scalar.activation, wt2[:, :], wt1[:, :], AF.Abs,
                  bias=0.0, scale=1.0)
                A(cx, nc.scalar.activation, wt2[:, :], wt2[:, :], AF.Exp,
                  bias=0.0, scale=-1.0)
                A(cx, nc.scalar.activation, wt2[:, :], wt2[:, :], AF.Ln,
                  bias=1.0, scale=1.0)
                A(cx, nc.scalar.activation, wt1[:, :], wt1[:, :], AF.Relu,
                  bias=0.0, scale=1.0)
                cx.act.wait(nc.vector)
                c_st.wait(nc.vector)
                V(cx, nc.vector.tensor_add, dt_f[:, :], wt2[:, :], wt1[:, :])
                V(cx, nc.vector.tensor_copy, dthl[:, :], dt_f[:, :])
                cx.dve.wait(nc.gpsimd)
                dma_out(cx, c_st, w["dthi_d"][dc, :, sl * 512:(sl + 1) * 512],
                        dthl[:, :])
                c_st.wait(nc.vector)
                V(cx, nc.vector.tensor_tensor, dthl[:, :], dt_f[:, :], dthl[:, :],
                  ALU.subtract)
                cx.dve.wait(nc.gpsimd)
                dma_out(cx, c_st, w["dtlo_d"][dc, :, sl * 512:(sl + 1) * 512],
                        dthl[:, :])
                c_xn2ld.wait(nc.vector)
                c_st.wait(nc.vector)
                V(cx, nc.vector.tensor_tensor, dthl[:, :], dt_f[:, :],
                  xn2ld[:, :], ALU.mult)
                cx.dve.wait(nc.gpsimd)
                dma_out(cx, c_st, w["dtx_d"][dc, :, sl * 512:(sl + 1) * 512],
                        dthl[:, :])
        c_st.wait(nc.sync)

        for dc in range(16):
            k = 0
            cx.pe.wait(nc.sync)
            dma_in(cx, c_dt3[k], dt3[k][:, 0:T], w["dthi_d"][dc])
            dma_in(cx, c_dt3[k], dt3[k][:, T:2 * T], w["dtlo_d"][dc])
            dma_in(cx, c_dt3[k], dt3[k][:, 2 * T:3 * T], w["dtx_d"][dc])
            c_dt3[k].wait(nc.tensor)

            def emit_reps(gp):
                kr = gp % 2
                dtr = dtrep[kr]
                dtxr = dtxrep[kr]
                for sl in range(nSL):
                    cx.act.wait(nc.tensor)
                    m = nc.tensor.matmul(ps_pj[:, :], Eg[:, gp * 128:(gp + 1) * 128],
                                         dt3[k][:, sl * 512:(sl + 1) * 512],
                                         start=True, stop=False)
                    m = nc.tensor.matmul(ps_pj[:, :], Eg[:, gp * 128:(gp + 1) * 128],
                                         dt3[k][:, T + sl * 512:T + (sl + 1) * 512],
                                         start=False, stop=True)
                    cx.pe.inc(m, 1)
                    cx.pe.wait(nc.scalar)
                    cx.dve.wait(nc.scalar, dve_after_scan.get(("rep", kr), 0))
                    A(cx, nc.scalar.activation, dtr[:, sl * 512:(sl + 1) * 512],
                      ps_pj[:, :], AF.Copy, bias=0.0, scale=1.0)
                    cx.act.wait(nc.tensor)
                    m = nc.tensor.matmul(ps_pj[:, :], Eg[:, gp * 128:(gp + 1) * 128],
                                         dt3[k][:, 2 * T + sl * 512:2 * T + (sl + 1) * 512],
                                         start=True, stop=True)
                    cx.pe.inc(m, 1)
                    cx.pe.wait(nc.scalar)
                    cx.dve.wait(nc.scalar, dve_after_scan.get(("rep", kr), 0))
                    A(cx, nc.scalar.activation, dtxr[:, sl * 512:(sl + 1) * 512],
                      ps_pj[:, :], AF.Copy, bias=0.0, scale=1.0)
                act_after_rep[gp % 2] = cx.act.val

            emit_reps(0)
            for gp in range(8):
                g = dc * 8 + gp
                kr = gp % 2
                dtr = dtrep[kr]
                dtxr = dtxrep[kr]

                def load_bc(jj):
                    kb = jj % 2
                    cx.dve.wait(nc.sync, dve_after_scan.get(("br", kb), 0))
                    dma_in(cx, c_br[kb], bct[kb][:, :], w["bcrep_d"][jj])

                if dc == 0 and gp == 0:
                    load_bc(0)
                act_rep_gate = act_after_rep[kr]
                for j in range(8):
                    if j == 3 and gp < 7:
                        emit_reps(gp + 1)
                    kb = j % 2
                    kA = j % 2
                    A_col = a_neg[:, g * 8 + j: g * 8 + j + 1]
                    cx.dve.wait(nc.scalar, dve_after_scan.get(("scan", kA), 0))
                    A(cx, nc.scalar.activation, dA_sb[kA][:, :], dtr[:, :], AF.Exp,
                      bias=0.0, scale=A_col)
                    c_br[kb].wait(nc.vector)
                    if j == 0:
                        cx.act.wait(nc.vector, act_rep_gate)
                    V(cx, nc.vector.tensor_tensor, u_sb[:, :], dtxr[:, :],
                      brt[kb][:, :], ALU.mult)
                    if j < 7:
                        load_bc(j + 1)
                    elif gp < 7 or dc < 15:
                        load_bc(0)
                    cx.act.wait(nc.vector)
                    kg = j % 2
                    V(cx, nc.vector.tensor_tensor_scan, h_sb[kg][:, :],
                      dA_sb[kA][:, :], u_sb[:, :], 0.0, ALU.mult, ALU.add)
                    dve_after_scan[("scan", kA)] = cx.dve.val
                    dve_after_scan[("br", kb)] = cx.dve.val
                    if j == 7:
                        dve_after_scan[("rep", kr)] = cx.dve.val
                    cx.pe.wait(nc.vector, pe_after_y.get(kg, 0))
                    V(cx, nc.vector.tensor_tensor, g_sb[kg][:, :], h_sb[kg][:, :],
                      crt[kb][:, :], ALU.mult)
                    dve_after_g[kg] = cx.dve.val
                    cx.dve.wait(nc.tensor, dve_after_g[kg])
                    for sl in range(nSL):
                        m = nc.tensor.matmul(
                            ps_y[sl][:, :], Ered[:, gp * 128:(gp + 1) * 128],
                            g_sb[kg][:, sl * 512:(sl + 1) * 512],
                            start=(gp == 0 and j == 0), stop=(gp == 7 and j == 7))
                        if sl == nSL - 1:
                            cx.pe.inc(m, 1)
                    pe_after_y[kg] = cx.pe.val
            cx.pe.wait(nc.vector)
            for p0 in range(0, nSL, 2):
                hs = range(min(2, nSL - p0))
                for h_ in hs:
                    sl = p0 + h_
                    cx.dve.wait(nc.sync)
                    dma_in(cx, c_x1dt[h_], x1dt[h_][:, :],
                           w["x1d_d"][dc, :, sl * 512:(sl + 1) * 512])
                for h_ in hs:
                    sl = p0 + h_
                    c_x1dt[h_].wait(nc.vector)
                    c_st.wait(nc.vector)
                    V(cx, nc.vector.tensor_tensor, pre_t[:, :], ps_y[sl][:, :],
                      x1dt[h_][:, :], ALU.add)
                    cx.dve.wait(nc.gpsimd)
                    dma_out(cx, c_st, w["pre_d"][dc, :, sl * 512:(sl + 1) * 512],
                            pre_t[:, :])
        c_st.wait(nc.sync)

        for eo in range(4):
            cx.pe.wait(nc.sync)
            for o in range(4):
                dma_in(cx, c_wo,
                       wslab[:, :].rearrange("p (c o f) -> p c o f", c=16, o=4)[:, :, o, :],
                       w["opwT"][eo * 4 + o].transpose([1, 0, 2]))
            c_wo.wait(nc.tensor)
            for tc in range(nTC):
                row0 = base + tc * 128
                kx = tc % 2
                cx.pe.wait(nc.sync)
                dma_in(cx, c_pre[0], preslab[0][:, :].rearrange("p (c t) -> p c t", c=16),
                       w["pre_d"][:, :, tc * 128:(tc + 1) * 128].transpose([1, 0, 2]))
                cx.dve.wait(nc.sync)
                dma_in(cx, c_res[0], res[0][:, :],
                       x_in[row0:row0 + 128, eo * 512:(eo + 1) * 512])
                c_pre[0].wait(nc.tensor)
                ps_o = ps_y[tc % 2]
                cx.dve.wait(nc.tensor, dve_after_c5.get(tc % 2, 0))
                for dc in range(16):
                    m = nc.tensor.matmul(
                        ps_o[:, :], preslab[0][:, dc * 128:(dc + 1) * 128],
                        wslab[:, dc * 512:(dc + 1) * 512],
                        start=(dc == 0), stop=(dc == 15))
                    if dc == 15:
                        cx.pe.inc(m, 1)
                cx.pe.wait(nc.vector)
                c_res[0].wait(nc.vector)
                c_out[0].wait(nc.vector)
                V(cx, nc.vector.tensor_tensor, out_t[0][:, :], ps_o[:, :],
                  res[0][:, :], ALU.add)
                dve_after_c5[tc % 2] = cx.dve.val
                cx.dve.wait(nc.gpsimd)
                dma_out(cx, c_out[0],
                        x_out[row0:row0 + 128, eo * 512:(eo + 1) * 512],
                        out_t[0][:, :])


def build_scan_program():
    nc = bass.Bass("TRN2", target_bir_lowering=False)
    x_in = nc.dram_tensor("x_in", [TOK, D], F32, kind="ExternalInput").ap()
    x_out = nc.dram_tensor("x_out", [TOK, D], F32, kind="ExternalOutput").ap()
    ident = nc.dram_tensor("ident", [128, 128], F32, kind="ExternalInput").ap()
    identb = nc.dram_tensor("identb", [128, 128], BF16, kind="ExternalInput").ap()
    w = declare_scan_io(nc)
    with ExitStack() as stack:
        cx = Ctx(nc, stack)
        cx.init_consts()
        ident_sb = cx.sb("ident_sb", [128, 128], F32)
        identb_sb = cx.sb("identb_sb", [128, 128], BF16)
        c0 = cx.dma_ctr("ident")
        dma_in(cx, c0, ident_sb[:, :], ident[:, :])
        dma_in(cx, c0, identb_sb[:, :], identb[:, :])
        c0.wait(nc.tensor)
        emit_scan_segment(cx, x_in, x_out, w, ident_sb, identb_sb)
        cx.final_barrier()
    return nc





def build_fused_program():
    nc = bass.Bass("TRN2", target_bir_lowering=False)
    x0 = nc.dram_tensor("x_in", [TOK, D], F32, kind="ExternalInput").ap()
    x4 = nc.dram_tensor("x_out", [TOK, D], F32, kind="ExternalOutput").ap()
    x1 = nc.dram_tensor("x1_res", [TOK, D], F32).ap()
    x2 = nc.dram_tensor("x2_res", [TOK, D], F32).ap()
    x3 = nc.dram_tensor("x3_res", [TOK, D], F32).ap()
    ident = nc.dram_tensor("ident", [128, 128], F32, kind="ExternalInput").ap()
    identb = nc.dram_tensor("identb", [128, 128], BF16, kind="ExternalInput").ap()
    wA = declare_attn_io(nc, prefix="A_")
    wB = declare_ffn_io(nc, 0, prefix="B_")
    wC = declare_scan_io(nc, prefix="C_")
    wD = declare_ffn_io(nc, 1, prefix="D_")
    hc = attn_host_constants()
    with ExitStack() as sem_stack, ExitStack() as const_stack:
        cx = Ctx(nc, const_stack, sem_stack=sem_stack)
        cx.init_consts()
        ident_sb = cx.sb("ident_sb", [128, 128], F32)
        identb_sb = cx.sb("identb_sb", [128, 128], BF16)
        c0 = cx.dma_ctr("ident")
        dma_in(cx, c0, ident_sb[:, :], ident[:, :])
        dma_in(cx, c0, identb_sb[:, :], identb[:, :])
        c0.wait(nc.tensor)
        c0.wait(nc.vector)
        c0.wait(nc.scalar)

        with ExitStack() as seg:
            cx.stack = seg
            emit_attn_segment(cx, x0, x1, wA, hc["r_C"], identb_sb)
            cx.segment_barrier()
        with ExitStack() as seg:
            cx.stack = seg
            emit_ffn_segment(cx, x1, x2, wB, ident_sb, identb_sb)
            cx.segment_barrier()
        with ExitStack() as seg:
            cx.stack = seg
            emit_scan_segment(cx, x2, x3, wC, ident_sb, identb_sb)
            cx.segment_barrier()
        with ExitStack() as seg:
            cx.stack = seg
            emit_ffn_segment(cx, x3, x4, wD, ident_sb, identb_sb)
            cx.segment_barrier()
        cx.stack = const_stack
        cx.final_barrier()
    return nc



_PROGS = {}


def _get_prog(name, builder):
    if name not in _PROGS:
        _PROGS[name] = builder()
    return _PROGS[name]


def kernel(**inputs):
    x = np.ascontiguousarray(inputs["L0_attn_x"], dtype=np.float32)
    shards = [x[2 * c:2 * c + 2].reshape(TOK, D) for c in range(NCORES)]

    hc = host_constants()
    ac = attn_host_constants()
    sc = scan_host_constants()
    com = {
        "ident": hc["ident"], "identb": hc["identb"],
        "A_Wq": inputs["L0_attn_Wq"], "A_Wk": inputs["L0_attn_Wk"],
        "A_Wv": inputs["L0_attn_Wv"], "A_Wo": inputs["L0_attn_Wo"],
        "A_rms_w": inputs["L0_attn_rms_w"],
        "A_qdecT": np.asarray(ac["qdecT"]), "A_kdecT": np.asarray(ac["kdecT"]),
        "A_dmaskT": np.asarray(ac["dmaskT"]), "A_cosT": np.asarray(ac["cosT"]),
        "A_sinS": np.asarray(ac["sinS"]), "A_Pswap": np.asarray(ac["Pswap"]),
        "B_gate_w": inputs["L0_ffn_gate_w"], "B_up_w": inputs["L0_ffn_up_w"],
        "B_down_w": inputs["L0_ffn_down_w"], "B_gate_s": inputs["L0_ffn_gate_s"],
        "B_up_s": inputs["L0_ffn_up_s"], "B_down_s": inputs["L0_ffn_down_s"],
        "C_norm_w": inputs["L1_attn_norm_w"], "C_norm_b": inputs["L1_attn_norm_b"],
        "C_x_proj_w": inputs["L1_attn_x_proj_w"],
        "C_dt_proj_w": inputs["L1_attn_dt_proj_w"],
        "C_dt_proj_b": inputs["L1_attn_dt_proj_b"],
        "C_A_log_sc": rearrange_A_log(np.asarray(inputs["L1_attn_A_log"])),
        "C_D_param": inputs["L1_attn_D_param"],
        "C_out_proj_w": inputs["L1_attn_out_proj_w"],
        "C_Eg": np.asarray(sc["Eg"]), "C_EB": np.asarray(sc["EB"]),
        "C_EC": np.asarray(sc["EC"]), "C_Ered": np.asarray(sc["Ered"]),
        "D_up_w": inputs["L1_ffn_up_weight"], "D_down_w": inputs["L1_ffn_down_weight"],
        "D_up_s": inputs["L1_ffn_up_scale"], "D_down_s": inputs["L1_ffn_down_scale"],
        "D_out_s": inputs["L1_ffn_output_scale"],
    }
    nc = _get_prog("fused", build_fused_program)
    res = run_bass_kernel_spmd(
        nc, [dict(com, x_in=s_) for s_ in shards], core_ids=list(range(NCORES)))
    shards = [r["x_out"] for r in res.results]
    out = np.stack([s_.reshape(2, T, D) for s_ in shards], 0).reshape(16, T, D)
    return out
